# version 26
# baseline (speedup 1.0000x reference)
import contextlib
import numpy as np
import concourse.bass as bass
import concourse.mybir as mybir
from concourse.bass_utils import run_bass_kernel_spmd

F32 = mybir.dt.float32
BF16 = mybir.dt.bfloat16
A = mybir.AluOpType
AF = mybir.ActivationFunctionType
AX = mybir.AxisListType.X
MAGIC = 12582912.0
QP = 127.0
EPS_LN = 1e-5
EPS_AT = 1e-6


class Sem:
    def __init__(self):
        self.h = None
        self.n = 0


class Eng:
    def __init__(self, name):
        self.name = name
        self.ops = []

    def do(self, fn):
        self.ops.append(fn)

    def dma(self, sem, out, in_, **kw):
        sem.n += 16
        self.ops.append(lambda E, o=out, i=in_, kw=kw, s=sem: E.dma_start(out=o, in_=i, **kw).then_inc(s.h, 16))
        return sem.n

    def dma_tr(self, sem, out, in_):
        sem.n += 16
        self.ops.append(lambda E, o=out, i=in_, s=sem: E.dma_start_transpose(o, i).then_inc(s.h, 16))
        return sem.n

    def wait(self, sem, v=None):
        v = sem.n if v is None else v
        if v > 0:
            self.ops.append(lambda E, s=sem, v=v: E.wait_ge(s.h, v))

    def run(self, E):
        for f in self.ops:
            f(E)


def build(T=4096, HID=4096, ncores=8, cut=None, dbg=False):
    TC = T // 128
    NT = T // 512
    HC = HID // 128
    D = 1024
    HCpad = ((HC + 15) // 16) * 16
    nc = bass.Bass("TRN2", target_bir_lowering=False)

    dxh = nc.dram_tensor("x", [T, D], F32, kind="ExternalInput")
    dwqh = nc.dram_tensor("wq", [D, D], F32, kind="ExternalInput")
    dwkh = nc.dram_tensor("wk", [D, D], F32, kind="ExternalInput")
    dwvh = nc.dram_tensor("wv", [D, D], F32, kind="ExternalInput")
    dwoh = nc.dram_tensor("wo", [D, D], F32, kind="ExternalInput")
    dbqh = nc.dram_tensor("bq", [1, D], F32, kind="ExternalInput")
    dbkh = nc.dram_tensor("bk", [1, D], F32, kind="ExternalInput")
    dbvh = nc.dram_tensor("bv", [1, D], F32, kind="ExternalInput")
    dboh = nc.dram_tensor("bo", [1, D], F32, kind="ExternalInput")
    drwh = nc.dram_tensor("rw", [16, 64, 64], F32, kind="ExternalInput")
    dw1h = nc.dram_tensor("w1", [HID, D], F32, kind="ExternalInput")
    db1h = nc.dram_tensor("b1", [1, HID], F32, kind="ExternalInput")
    dw2h = nc.dram_tensor("w2", [D, HID], F32, kind="ExternalInput")
    db2h = nc.dram_tensor("b2", [1, D], F32, kind="ExternalInput")
    douth = nc.dram_tensor("out", [T, D], F32, kind="ExternalOutput")
    dx, dwq, dwk, dwv, dwo = dxh.ap(), dwqh.ap(), dwkh.ap(), dwvh.ap(), dwoh.ap()
    dbq, dbk, dbv, dbo, drw = dbqh.ap(), dbkh.ap(), dbvh.ap(), dboh.ap(), drwh.ap()
    dw1, db1, dw2, db2, dout = dw1h.ap(), db1h.ap(), dw2h.ap(), db2h.ap(), douth.ap()

    KD = dict(kind="ExternalOutput") if dbg else {}
    s_wqh = nc.dram_tensor("s_wq", [D, D], BF16)
    s_wkh = nc.dram_tensor("s_wk", [D, D], BF16)
    s_smh = nc.dram_tensor("s_small", [64, 128], BF16)
    s_ln1h = nc.dram_tensor("s_ln1", [T, D], BF16, **KD)
    s_kph = nc.dram_tensor("s_kp", [D, T], BF16, **KD)
    s_qph = nc.dram_tensor("s_qp", [D, T], BF16, **KD)
    s_rch = nc.dram_tensor("s_recip", [16, T], F32, **KD)
    s_x1h = nc.dram_tensor("s_x1", [T, D], F32, **KD)
    s_xqh = nc.dram_tensor("s_xq", [T, D], BF16, **KD)
    s_w1qh = nc.dram_tensor("s_w1q", [HID, D], BF16, **KD)
    s_w1th = nc.dram_tensor("s_w1qt", [D, HID], BF16)
    s_w2qh = nc.dram_tensor("s_w2q", [D, HID], BF16, **KD)
    s_hh = nc.dram_tensor("s_h", [HID, T], F32, **KD)
    s_flh = nc.dram_tensor("s_flags", [8, 8], F32)
    d_oneh = nc.dram_tensor("d_one", [1, 64], BF16)
    cc1i = nc.dram_tensor("cc1i", [1, 8], F32)
    cc1o = nc.dram_tensor("cc1o", [1, 8], F32)
    cc2i = nc.dram_tensor("cc2i", [1, 8], F32)
    cc2o = nc.dram_tensor("cc2o", [1, 8], F32)
    s_wq, s_wk, s_sm, s_ln1 = s_wqh.ap(), s_wkh.ap(), s_smh.ap(), s_ln1h.ap()
    s_kp, s_qp, s_rc, s_x1 = s_kph.ap(), s_qph.ap(), s_rch.ap(), s_x1h.ap()
    s_xq, s_w1q, s_w1t, s_w2q = s_xqh.ap(), s_w1qh.ap(), s_w1th.ap(), s_w2qh.ap()
    s_h, s_fl = s_hh.ap(), s_flh.ap()
    d_one = d_oneh.ap()

    al = nc.alloc_sbuf_tensor
    A1C = max(8 * T, 16384)
    a1h = al("a1", [128, A1C], BF16)
    wqrh = al("wqr", [128, 8192], BF16)
    wkrh = al("wkr", [128, 8192], BF16)
    wvoh = al("wvo", [128, 8192], BF16)
    ringh = al("ring", [128, 8320], BF16)
    fbufh = al("fbuf", [128, 4096], F32)
    bfrh = al("bfr", [128, 2048], BF16)
    zbh = al("zb", [128, 1024], F32)
    obh = al("ob", [128, 2048], F32)
    a1, wqr, wkr, wvo = a1h.ap(), wqrh.ap(), wkrh.ap(), wvoh.ap()
    ring, fbuf, bfr, zb, ob = ringh.ap(), fbufh.ap(), bfrh.ap(), zbh.ap(), obh.ap()
    bo_b = al("bo_b", [128, 1024], F32).ap()
    b2_b = al("b2_b", [128, 1024], F32).ap()
    rcpr = al("rcpr", [16, 512], F32).ap()
    selb = al("selb", [16, 1024], BF16).ap()
    onesb = al("onesb", [1, 64], BF16).ap()
    rcpb = al("rcpb", [128, 1024], F32).ap()
    rwp = al("rwp", [128, 512], BF16).ap()
    bqp = al("bqp", [128, 16], BF16).ap()
    bkp = al("bkp", [128, 16], BF16).ap()
    b1c = al("b1c", [128, HCpad], BF16).ap()
    bvp = al("bvp", [128, 512], BF16).ap()
    kvsb = al("kvsb", [128, 520], BF16).ap()
    rcb = kvsb
    kvf = al("kvf", [128, 512], BF16).ap()
    ksum = al("ksum", [128, 8], F32).ap()
    scb = al("scb", [128, 128], BF16).ap()
    qpr = al("qpr", [128, 4096], BF16).ap()
    st_s = al("st_s", [128, TC], F32).ap()
    st_q = al("st_q", [128, TC], F32).ap()
    st_m = al("st_m", [128, TC], F32).ap()
    st_r = al("st_r", [128, TC], F32).ap()
    st_nm = al("st_nm", [128, TC], F32).ap()
    st_t1 = al("st_t1", [128, TC], F32).ap()
    st_t2 = al("st_t2", [128, TC], F32).ap()
    x_s = al("x_s", [128, TC], F32).ap()
    x_q = al("x_q", [128, TC], F32).ap()
    x_mx = al("x_mx", [128, TC], F32).ap()
    x_mn = al("x_mn", [128, TC], F32).ap()
    x_m = al("x_m", [128, TC], F32).ap()
    x_r = al("x_r", [128, TC], F32).ap()
    x_v = al("x_v", [128, TC], F32).ap()
    x_nm = al("x_nm", [128, TC], F32).ap()
    rs2 = al("rs2", [128, TC], F32).ap()
    boff = al("boff", [128, TC], F32).ap()
    c1b1 = al("c1b1", [128, HCpad], F32).ap()
    wmax = al("wmax", [128, 32], F32).ap()
    hmaxs = al("hmaxs", [128, max(HC * NT, 8)], F32).ap()
    g128 = al("g128", [1, 128], F32).ap()
    sc1 = al("sc1", [1, 16], F32).ap()
    ccs = al("ccs", [1, 8], F32).ap()
    ppc = al("ppc", [128, 8], F32).ap()
    zrow = al("zrow", [1, 1024], BF16).ap()
    brow = al("brow", [1, max(HID, 1024)], BF16).ap()
    wqb = al("wqb", [128, 2048], BF16).ap()
    cst = al("cst", [128, 2], F32).ap()
    bqr_sb = al("bqr_sb", [128, 8], F32).ap()
    bkr_sb = al("bkr_sb", [128, 8], F32).ap()

    ap_ = nc.alloc_psum_tensor
    pA = ap_("pA", [128, 512], F32).ap()
    pB = ap_("pB", [128, 512], F32).ap()
    pC = ap_("pC", [128, 512], F32).ap()
    pD = ap_("pD", [128, 512], F32).ap()
    pK1h = ap_("pK1", [128, 512], F32)
    pK2h = ap_("pK2", [128, 512], F32)
    pK1, pK2 = pK1h.ap(), pK2h.ap()
    pN = ap_("pN", [128, 512], F32).ap()
    pE = ap_("pE", [128, 512], F32).ap()

    gp, sy, tn, sc, vc = Eng("gp"), Eng("sy"), Eng("tn"), Eng("sc"), Eng("vc")
    phase_marks = {}
    def phase(name):
        if name not in phase_marks:
            phase_marks[name] = tuple(len(e.ops) for e in (gp, sy, tn, sc, vc))
    tns, scs, vcs, ccsm = Sem(), Sem(), Sem(), Sem()
    gpe, sye = Sem(), Sem()
    gpA, gpB, gpQ0, gpQ1, gpH0, gpH1, gpR, gpP, gpP2a, gpP2b, gpC = (
        Sem(), Sem(), Sem(), Sem(), Sem(), Sem(), Sem(), Sem(), Sem(), Sem(), Sem())
    scL0, scL1, scH, scX0, scX1 = Sem(), Sem(), Sem(), Sem(), Sem()
    syA, syB, syK0, syK1, syX0, syX1, syW0, syW1, syS0, syS1, syC = (
        Sem(), Sem(), Sem(), Sem(), Sem(), Sem(), Sem(), Sem(), Sem(), Sem(), Sem())
    syT = [Sem(), Sem(), Sem(), Sem()]
    ALL_SEMS = [(tns, "tns"), (scs, "scs"), (vcs, "vcs"), (ccsm, "ccsm"),
                (scL0, "scL0"), (scL1, "scL1"), (scH, "scH"), (scX0, "scX0"), (scX1, "scX1"),
                (gpe, "gpe"), (sye, "sye"),
                (gpA, "gpA"), (gpB, "gpB"), (gpQ0, "gpQ0"), (gpQ1, "gpQ1"),
                (gpH0, "gpH0"), (gpH1, "gpH1"), (gpR, "gpR"), (gpP, "gpP"),
                (gpP2a, "gpP2a"), (gpP2b, "gpP2b"), (gpC, "gpC"),
                (syA, "syA"), (syB, "syB"), (syK0, "syK0"), (syK1, "syK1"),
                (syX0, "syX0"), (syX1, "syX1"), (syW0, "syW0"), (syW1, "syW1"),
                (syS0, "syS0"), (syS1, "syS1"), (syC, "syC"),
                (syT[0], "syT0"), (syT[1], "syT1"), (syT[2], "syT2"), (syT[3], "syT3")]

    def gp_mark():
        for s_ in (gpA, gpB, gpQ0, gpQ1, gpH0, gpH1, gpR, gpP, gpP2a, gpP2b, gpC):
            gp.wait(s_)
        gpe.n += 1
        gp.do(lambda E, v=gpe.n: E.nop().then_inc(gpe.h, 1))
        return gpe.n

    def sy_mark(sems=None):
        for s_ in (sems if sems is not None else
                   (syA, syB, syK0, syK1, syX0, syX1, syW0, syW1, syS0, syS1, syC,
                    syT[0], syT[1], syT[2], syT[3])):
            sy.wait(s_)
        sye.n += 1
        sy.do(lambda E, v=sye.n: E.nop().then_inc(sye.h, 1))
        return sye.n

    def mm(p_, lhsT, rhs, start, stop, inc=False, tp=None, skip=False):
        if inc:
            tns.n += 1
        def f(E, p_=p_, l=lhsT, r=rhs, st=start, sp=stop, tp=tp, inc=inc, sk=skip):
            i = E.matmul(p_, l, r, start=st, stop=sp, tile_position=tp, skip_group_check=sk)
            if inc:
                i.then_inc(tns.h, 1)
        tn.ops.append(f)
        return tns.n

    def v_op(name, *args, inc=True, **kw):
        prev = vcs.n
        vcs.n += 1
        def f(E, name=name, args=args, kw=kw, prev=prev):
            if prev > 0:
                E.wait_ge(vcs.h, prev)
            i = getattr(E, name)(*args, **kw)
            i.then_inc(vcs.h, 1)
        vc.ops.append(f)
        return vcs.n

    def s_op(name, *args, inc=True, **kw):
        prev = scs.n
        scs.n += 1
        def f(E, name=name, args=args, kw=kw, prev=prev):
            if prev > 0:
                E.wait_ge(scs.h, prev)
            i = getattr(E, name)(*args, **kw)
            i.then_inc(scs.h, 1)
        sc.ops.append(f)
        return scs.n


    v_op("memset", cst[:, 0:1], EPS_LN, inc=False)
    v_op("memset", sc1[:, :], 0.0, inc=False)
    v_op("memset", ccs[:, :], 0.0, inc=False)
    v_op("memset", selb[:, :], 0.0, inc=False)
    v_op("memset", onesb[0:1, :], 1.0, inc=False)
    v_cst = v_op("memset", cst[:, 1:2], MAGIC)
    gp.dma(gpC, rwp[0:64, :], bass.AP(drwh, 0, [[64, 64], [8192, 8], [1, 64]]))
    gp.dma(gpC, rwp[64:128, :], bass.AP(drwh, 4096, [[64, 64], [8192, 8], [1, 64]]))
    gpe.n += 1
    gp.do(lambda E: E.memset(zrow[:, :], 0.0).then_inc(gpe.h, 1))
    gpe.n += 1
    gp.do(lambda E: E.memset(brow[:, :], 0.0).then_inc(gpe.h, 1))
    gp.wait(gpe, gpe.n)

    def stage_bias(dsrc, n_elem, row0):
        gp.wait(gpC)
        gp.dma(gpC, brow[0:1, 0:n_elem], dsrc[0:1, 0:n_elem])
        gp.wait(gpC)
        nrow = n_elem // 128
        gp.dma(gpC, s_sm[row0:row0 + nrow, :], brow[0:1, 0:n_elem])
        pad = (-nrow) % 16
        if pad:
            gp.dma(gpC, s_sm[row0 + nrow:row0 + nrow + pad, :], zrow[0:1, 0:pad * 128])

    phase('P0a')
    stage_bias(dbq, 1024, 0)
    stage_bias(dbk, 1024, 16)
    stage_bias(db1, HID, 32)
    gp.dma(gpC, bo_b[:, :], dbo[0:1, :].partition_broadcast(128))
    gp.dma(gpC, b2_b[:, :], db2[0:1, :].partition_broadcast(128))
    gp.dma(gpC, bvp[0:64, :], bass.AP(dbvh, 0, [[0, 64], [128, 8], [1, 64]]))
    gp.dma(gpC, bvp[64:128, :], bass.AP(dbvh, 64, [[0, 64], [128, 8], [1, 64]]))
    gp.wait(vcs, v_cst)
    gp.dma(gpC, d_one[0:1, :], onesb[0:1, :])
    gp.wait(gpC)
    for c_ in range(8):
        gp.dma(gpC, selb[2 * c_:2 * c_ + 1, c_ * 128:c_ * 128 + 64], d_one[0:1, :])
        gp.dma(gpC, selb[2 * c_ + 1:2 * c_ + 2, c_ * 128 + 64:c_ * 128 + 128],
               d_one[0:1, :])
    m_setup = gp_mark()

    phase('P0b')
    sy.wait(gpe, m_setup)
    sy.dma_tr(syC, bqp[:, :], s_sm[0:16, :])
    sy.dma_tr(syC, bkp[:, :], s_sm[16:32, :])
    sy.dma_tr(syC, b1c[:, :], s_sm[32:32 + HCpad, :])
    m_bias = sy_mark([syC])

    def stage_w(dsrc, sdst):
        gp.wait(gpC)
        phase('Pc2')
        for i in range(8):
            gp.dma(gpC, a1[:, i * 1024:(i + 1) * 1024], dsrc[i * 128:(i + 1) * 128, :])
        phase('Pc3')
        gp.wait(gpC)
        for i in range(8):
            gp.dma(gpC, sdst[i * 128:(i + 1) * 128, :], a1[:, i * 1024:(i + 1) * 1024])
        phase('Pc4')
        return gp_mark()

    def prep_wr(m_stage, sdst, wrdst, bias_pair, bqr_dst):
        phase('Pd1')
        sy.wait(gpe, m_stage)
        sy.wait(tns)
        for ec in range(8):
            sy.dma_tr(syC, a1[:, 8192 + ec * 1024:8192 + (ec + 1) * 1024],
                      sdst[:, ec * 128:(ec + 1) * 128])
        m_wt = sy_mark([syC])
        phase('Pd2')
        tn.wait(sye, m_wt)
        tn.wait(gpe, m_setup)
        wrh = wrdst.copy() if False else wrdst
        for dc in range(8):
            tn.wait(scs)
            for c in range(8):
                base = 8192 + c * 1024 + dc * 128
                mm(pA[:, c * 64:(c + 1) * 64], a1[0:64, base:base + 128],
                   rwp[0:64, c * 64:(c + 1) * 64], True, True)
            for c in range(8):
                base = 8192 + c * 1024 + dc * 128
                mm(pB[:, c * 64:(c + 1) * 64], a1[64:128, base:base + 128],
                   rwp[64:128, c * 64:(c + 1) * 64], True, True, inc=(c == 7))
            sc.wait(tns)
            dste = bass.AP(wrdst.tensor, wrdst.offset + dc * 1024,
                           [[wrdst.ap[0][0], 128], [128, 8], [1, 64]])
            dsto = bass.AP(wrdst.tensor, wrdst.offset + dc * 1024 + 64,
                           [[wrdst.ap[0][0], 128], [128, 8], [1, 64]])
            s_op("copy", dste, pA)
            s_op("copy", dsto, pB)
        phase('Pd3')
        tn.wait(vcs)
        tn.wait(sye, m_bias)
        for c in range(8):
            mm(pN[0:64, c:c + 1], rwp[0:64, c * 64:(c + 1) * 64], bias_pair[0:64, c:c + 1],
               c == 0, c == 7, skip=True)
            mm(pN[64:128, c:c + 1], rwp[64:128, c * 64:(c + 1) * 64], bias_pair[64:128, c:c + 1],
               c == 0, c == 7, inc=(c == 7), tp=(64, 64), skip=True)
        vc.wait(tns)
        v_op("tensor_copy", bqr_dst[:, :], pN[:, 0:8])

    phase('P0c')
    sc.wait(vcs, v_cst)
    phase('P0c5')
    m_wq = stage_w(dwq, s_wq)
    prep_wr(m_wq, s_wq, wqr, bqp, bqr_sb)
    phase('P0d')
    m_wk = stage_w(dwk, s_wk)
    prep_wr(m_wk, s_wk, wkr, bkp, bkr_sb)
    gp.wait(gpC)
    for i in range(8):
        gp.dma(gpC, wvo[:, i * 1024:(i + 1) * 1024], dwv[i * 128:(i + 1) * 128, :])
    m_wv = gp_mark()
    tn_prep = tns.n
    s_prep = scs.n

    NW1 = HID // 128
    NW2q = (D // 128) * (HID // 1024)

    NW1 = HID // 128
    NW2q = (D // 128) * (HID // 1024)

    wq_state = {}

    def absmax_w1_gen():
        slots = [ob[:, 0:1024], ob[:, 1024:2048]]
        ssem = [gpH0, gpH1]
        v_wm = {}
        for i in range(NW1):
            wb = slots[i % 2]
            gx = ssem[i % 2]
            if i >= 2:
                gp.wait(vcs, v_wm[i - 2])
            g = gp.dma(gx, wb, dw1[i * 128:(i + 1) * 128, :])
            vc.wait(gx, g)
            v_wm[i] = v_op("reduce_max", wmax[:, i:i + 1], wb, AX, apply_absolute_value=True)
            yield
        v_op("reduce_max", wmax[:, 0:1], wmax[:, 0:NW1], AX)
        sy.wait(vcs)
        sy.dma(syC, g128[:, :], wmax[:, 0:1])
        m_g2 = sy_mark([syC])
        vc.wait(sye, m_g2)
        v_op("reduce_max", sc1[0:1, 0:1], g128[:, :], AX, inc=False)
        v_op("reciprocal", sc1[0:1, 2:3], sc1[0:1, 0:1], inc=False)
        v_op("tensor_scalar", sc1[0:1, 4:5], sc1[0:1, 2:3], QP, None, A.mult, inc=False)
        v_op("tensor_scalar", sc1[0:1, 6:7], sc1[0:1, 0:1], 1.0 / QP, None, A.mult)
        sy.wait(vcs)
        sy.dma(syC, s_fl[0:1, :], sc1[0:1, 0:8])
        m_fl0 = sy_mark([syC])
        gp.wait(sye, m_fl0)
        gp.wait(gpC)
        gp.dma(gpC, ppc[:, 4:5], s_fl[0:1, 4:5].partition_broadcast(128))
        wq_state['m_ppc1'] = gp_mark()
        wq_state['aw1_v'] = vcs.n

    def absmax_w2_gen():
        slots = [ob[:, 0:1024], ob[:, 1024:2048]]
        ssem = [gpQ0, gpQ1]
        v_wm2 = {}
        idx = 0
        for i in range(D // 128):
            for quar in range(HID // 1024):
                wb = slots[idx % 2]
                gx = ssem[idx % 2]
                if idx >= 2:
                    gp.wait(vcs, v_wm2[idx - 2])
                else:
                    gp.wait(vcs, wq_state['aw1_v'])
                g = gp.dma(gx, wb, dw2[i * 128:(i + 1) * 128, quar * 1024:(quar + 1) * 1024])
                vc.wait(gx, g)
                v_wm2[idx] = v_op("reduce_max", wmax[:, idx:idx + 1], wb, AX,
                                  apply_absolute_value=True)
                idx += 1
                yield
        v_op("reduce_max", st_t2[:, 1:2], wmax[:, 0:NW2q], AX)
        sy.wait(vcs)
        sy.dma(syC, g128[:, :], st_t2[:, 1:2])
        m_g3 = sy_mark([syC])
        vc.wait(sye, m_g3)
        v_op("reduce_max", sc1[0:1, 1:2], g128[:, :], AX, inc=False)
        v_op("reciprocal", sc1[0:1, 3:4], sc1[0:1, 1:2], inc=False)
        v_op("tensor_scalar", sc1[0:1, 5:6], sc1[0:1, 3:4], QP, None, A.mult, inc=False)
        v_op("tensor_scalar", sc1[0:1, 7:8], sc1[0:1, 1:2], 1.0 / QP, None, A.mult)
        sy.wait(vcs)
        sy.dma(syC, s_fl[3:4, :], sc1[0:1, 0:8])
        m_fl3 = sy_mark([syC])
        gp.wait(sye, m_fl3)
        gp.wait(gpC)
        gp.dma(gpC, ppc[:, 5:6], s_fl[3:4, 5:6].partition_broadcast(128))
        wq_state['m_ppc2'] = gp_mark()
        wq_state['aw2_v'] = vcs.n

    def round_w1_gen():
        m_ppc = wq_state['m_ppc1']
        s_w1 = {}
        v_w1 = {}
        sy_w1 = {}
        for i in range(NW1):
            wb = fbuf[:, (i % 2) * 1024:(i % 2) * 1024 + 1024]
            gx = gpH0 if i % 2 == 0 else gpH1
            if i == 0:
                gp.wait(vcs, wq_state['v_fbuf'])
                gp.wait(scs, wq_state['s_fbuf'])
                gp.wait(gpA)
                gp.wait(gpB)
                gp.wait(gpQ0)
                gp.wait(gpQ1)
            if i >= 2:
                gp.wait(scs, s_w1[i - 2])
            g = gp.dma(gx, wb, dw1[i * 128:(i + 1) * 128, :])
            sc.wait(gx, g)
            sc.wait(gpe, m_ppc)
            tb = fbuf[:, 2048 + (i % 2) * 1024:2048 + (i % 2) * 1024 + 1024]
            if i >= 2:
                sc.wait(vcs, v_w1[i - 2])
            s_w1[i] = s_op("activation", tb, wb, AF.Identity, bias=cst[:, 1:2],
                           scale=ppc[:, 4:5])
            vc.wait(scs, s_w1[i])
            qb = wqb[:, (i % 2) * 1024:(i % 2) * 1024 + 1024]
            sw = syW0 if i % 2 == 0 else syW1
            if i >= 2:
                vc.wait(sw, sy_w1[i - 2])
            v_w1[i] = v_op("tensor_scalar", qb, tb, -MAGIC, None, A.add)
            sy.wait(vcs, v_w1[i])
            sy_w1[i] = sy.dma(sw, s_w1q[i * 128:(i + 1) * 128, :], qb)
            yield
        wq_state['w1_s'] = scs.n
        wq_state['w1_v'] = vcs.n
        wq_state['w1_sy'] = (syW0.n, syW1.n)

    def round_w2_gen():
        m_ppc = wq_state['m_ppc2']
        s_w2 = {}
        v_w2 = {}
        sy_w2 = {}
        idx = 0
        for i in range(D // 128):
            for quar in range(HID // 1024):
                wb = fbuf[:, 2048 + (idx % 2) * 1024:2048 + (idx % 2) * 1024 + 1024]
                if idx == 0:
                    sc.wait(vcs, wq_state['w1_v'])
                    sc.wait(scs, wq_state['w1_s'])
                    sc.wait(gpH0)
                    sc.wait(gpH1)
                if idx >= 2:
                    sc.wait(scs, s_w2[idx - 2])
                g = sc.dma(scH, wb, dw2[i * 128:(i + 1) * 128, quar * 1024:(quar + 1) * 1024])
                sc.wait(scH, g)
                sc.wait(gpe, m_ppc)
                tb = rcpb[:, 0:1024]
                if idx >= 1:
                    sc.wait(vcs, v_w2[idx - 1])
                s_w2[idx] = s_op("activation", tb, wb, AF.Identity, bias=cst[:, 1:2],
                                 scale=ppc[:, 5:6])
                vc.wait(scs, s_w2[idx])
                qb = wqb[:, (idx % 2) * 1024:(idx % 2) * 1024 + 1024]
                sw = syW0 if idx % 2 == 0 else syW1
                if idx == 0:
                    vc.wait(syW0, wq_state['w1_sy'][0])
                    vc.wait(syW1, wq_state['w1_sy'][1])
                if idx >= 2:
                    vc.wait(sw, sy_w2[idx - 2])
                v_w2[idx] = v_op("tensor_scalar", qb, tb, -MAGIC, None, A.add)
                sy.wait(vcs, v_w2[idx])
                sy_w2[idx] = sy.dma(sw, s_w2q[i * 128:(i + 1) * 128,
                                              quar * 1024:(quar + 1) * 1024], qb)
                idx += 1
                yield

    gen_a1 = absmax_w1_gen()
    gen_a2 = absmax_w2_gen()
    gen_r1 = None
    gen_r2 = None

    def wq_step():
        try:
            next(gen_a1)
        except StopIteration:
            pass

    def wq_step_b3(i):
        g1 = gen_a2 if i % 2 == 0 else wq_state['gr1']
        g2 = wq_state['gr1'] if i % 2 == 0 else gen_a2
        try:
            next(g1)
            return
        except StopIteration:
            pass
        try:
            next(g2)
        except StopIteration:
            pass

    def wq_step_b5():
        try:
            next(wq_state['gr2'])
        except StopIteration:
            pass

    phase('B1')
    for j in range(4):
        off = 4096 + j * 1040 + 64
        v_op("memset", bass.AP(ringh, off, [[8320, 128], [65, 16], [1, 1]]), 1.0,
             inc=(j == 3))
    v_ones = vcs.n

    v_r1 = {}
    s_sq = {}
    v_ln = {}
    s_ln = {}
    m_w = {}
    s_kpe = {}
    v_kp = {}
    sy_kps = {}
    kv_done = {}
    kst0 = {}
    kst1 = {}

    g_x = {}

    def b1_load(w):
        for t in range(4 * w, 4 * w + 4):
            xb = fbuf[:, (t % 4) * 1024:(t % 4) * 1024 + 1024]
            gx = (gpA, gpB, gpQ0, gpQ1)[t % 4]
            if t >= 4:
                gp.wait(vcs, v_ln[t - 4])
                gp.wait(scs, s_sq[t - 4])
            g_x[t] = gp.dma(gx, xb, dx[t * 128:(t + 1) * 128, :])

    def b1_pass1(w):
        for t in range(4 * w, 4 * w + 4):
            xb = fbuf[:, (t % 4) * 1024:(t % 4) * 1024 + 1024]
            gx = (gpA, gpB, gpQ0, gpQ1)[t % 4]
            vc.wait(gx, g_x[t])
            v_r1[t] = v_op("reduce_sum", st_s[:, t:t + 1], xb, AX)
            sc.wait(gx, g_x[t])
            s_sq[t] = s_op("activation", zb[:, :], xb, AF.Square,
                           accum_out=st_q[:, t:t + 1])

    def b1_pass2(w):
        c0, c1_ = 4 * w, 4 * w + 4
        vc.wait(scs, s_sq[4 * w + 3])
        v_op("tensor_scalar", st_m[:, c0:c1_], st_s[:, c0:c1_], 1.0 / 1024, None, A.mult,
             inc=False)
        v_op("tensor_tensor", st_t2[:, c0:c1_], st_m[:, c0:c1_], st_m[:, c0:c1_], A.mult,
             inc=False)
        v_st = v_op("scalar_tensor_tensor", st_t1[:, c0:c1_], st_q[:, c0:c1_], 1.0 / 1024,
                    st_t2[:, c0:c1_], A.mult, A.subtract)
        sc.wait(vcs, v_st)
        s_sd = s_op("activation", st_t2[:, c0:c1_], st_t1[:, c0:c1_], AF.Sqrt,
                    bias=cst[:, 0:1])
        vc.wait(scs, s_sd)
        v_op("reciprocal", st_r[:, c0:c1_], st_t2[:, c0:c1_], inc=False)
        v_op("scalar_tensor_tensor", st_nm[:, c0:c1_], st_m[:, c0:c1_], -1.0,
             st_r[:, c0:c1_], A.mult, A.mult)
        for t in range(4 * w, 4 * w + 4):
            xb = fbuf[:, (t % 4) * 1024:(t % 4) * 1024 + 1024]
            lb = wqb[:, (t % 2) * 1024:(t % 2) * 1024 + 1024]
            sl = scL0 if t % 2 == 0 else scL1
            if t >= 2:
                vc.wait(sl, s_ln[t - 2])
            v_ln[t] = v_op("tensor_scalar", lb, xb, st_r[:, t:t + 1], st_nm[:, t:t + 1],
                           A.mult, A.add)
            sc.wait(vcs, v_ln[t])
            s_ln[t] = sc.dma(sl, s_ln1[t * 128:(t + 1) * 128, :], lb)
        sy.wait(scL0)
        sy.wait(scL1)
        if w == 0:
            sy.wait(tns, tn_prep)
        for dc in range(8):
            sy.dma_tr(syC, a1[:, dc * T + w * 512:dc * T + (w + 1) * 512],
                      s_ln1[w * 512:(w + 1) * 512, dc * 128:(dc + 1) * 128])
        m_w[w] = sy_mark([syC])

    def ln1T(dc, c0, c1_):
        return a1[:, dc * T + c0:dc * T + c1_]

    def kp_window(tt):
        tn.wait(sye, m_w[tt])
        if tt == 0:
            tn.wait(scs, s_prep)
        else:
            tn.wait(vcs, v_vev[4 * tt - 1])
        for n in range(8):
            idx = tt * 8 + n
            p_ = (pA, pB, pC, pD)[idx % 4]
            if idx >= 4:
                tn.wait(scs, s_kpe[idx - 4])
            for dc in range(8):
                mm(p_, wkr[:, dc * 1024 + n * 128:dc * 1024 + (n + 1) * 128],
                   ln1T(dc, tt * 512, (tt + 1) * 512), dc == 0, dc == 7, inc=(dc == 7))
            sc.wait(tns)
            kb = bfr[:, (idx % 2) * 1024:(idx % 2) * 1024 + 512]
            ke = bfr[:, (idx % 2) * 1024 + 512:(idx % 2) * 1024 + 1024]
            if idx >= 2:
                sc.wait(vcs, v_kp[idx - 2])
            s_kpe[idx] = s_op("activation", kb, p_, AF.Relu, bias=bkr_sb[:, n:n + 1])
            vc.wait(scs, s_kpe[idx])
            sk = syK0 if idx % 2 == 0 else syK1
            if idx >= 2:
                vc.wait(sk, sy_kps[idx - 2])
            v_kp[idx] = v_op("tensor_scalar", ke, kb, EPS_AT, None, A.add)
            sy.wait(vcs, v_kp[idx])
            sy_kps[idx] = sy.dma(sk, s_kp[n * 128:(n + 1) * 128, tt * 512:(tt + 1) * 512],
                                 ke)
            wq_step()
        kst0[tt] = syK0.n
        kst1[tt] = syK1.n

    kv_first = [[True, True], [True, True]]
    v_vev = {}

    def emit_v(tc_):
        slot = tc_ % 4
        vs0 = 4096 + slot * 1040
        pv0 = (pC, pE, pA)[tc_ % 3]
        pv1 = (pD, pN, pB)[tc_ % 3]
        if tc_ == 0:
            tn.wait(gpe, m_wv)
        if tc_ % 4 == 0:
            tn.wait(scs, s_kpe[(tc_ // 4) * 8 + 7])
        if tc_ >= 3:
            tn.wait(vcs, v_vev[tc_ - 3])
        for dc in range(8):
            stat = ln1T(dc, tc_ * 128, (tc_ + 1) * 128)
            mm(pv0, stat, wvo[:, dc * 1024:dc * 1024 + 512], dc == 0, dc == 7)
            mm(pv1, stat, wvo[:, dc * 1024 + 512:(dc + 1) * 1024], dc == 0, dc == 7,
               inc=(dc == 7))
        vc.wait(tns)
        vc.wait(vcs, v_ones)
        if tc_ >= 4:
            vc.wait(tns, kv_done[tc_ - 4])
        v_op("tensor_copy", bass.AP(ringh, vs0, [[8320, 128], [65, 8], [1, 64]]), pv0,
             inc=False)
        v_vev[tc_] = v_op("tensor_copy",
                          bass.AP(ringh, vs0 + 8 * 65, [[8320, 128], [65, 8], [1, 64]]), pv1)

    def emit_kv(tc_):
        slot = tc_ % 4
        tn.wait(vcs, v_vev[tc_])
        tn.wait(syT[slot])
        for c in range(8):
            bk_ = 0 if c < 4 else 1
            pk = pK1 if c < 4 else pK2
            off = (c % 4) * 65
            st0 = kv_first[bk_][0]
            st1 = kv_first[bk_][1]
            kv_first[bk_] = [False, False]
            last = (tc_ == TC - 1) and (c % 4 == 3)
            kslot = ring[:, slot * 1024:(slot + 1) * 1024]
            vs = ring[:, 4096 + slot * 1040:4096 + (slot + 1) * 1040]
            mm(pk[0:64, off:off + 65], kslot[:, (2 * c) * 64:(2 * c + 1) * 64],
               vs[:, (2 * c) * 65:(2 * c + 1) * 65], st0, last, skip=True)
            mm(pk[64:128, off:off + 65], kslot[:, (2 * c + 1) * 64:(2 * c + 2) * 64],
               vs[:, (2 * c + 1) * 65:(2 * c + 2) * 65], st1, last,
               inc=(c == 7), tp=(0, 64), skip=True)
        kv_done[tc_] = tns.n

    def emit_tr(tc_):
        slot = tc_ % 4
        ttk = tc_ // 4
        sy.wait(syK0, kst0[ttk])
        sy.wait(syK1, kst1[ttk])
        if tc_ >= 4:
            sy.wait(tns, kv_done[tc_ - 4])
        sy.dma_tr(syT[slot], ring[:, slot * 1024:(slot + 1) * 1024],
                  s_kp[:, tc_ * 128:(tc_ + 1) * 128])

    phase('B2')
    b1_load(0)
    b1_pass1(0)
    b1_pass2(0)
    if NT > 1:
        b1_load(1)
    for w in range(NT):
        if w + 1 < NT:
            b1_pass1(w + 1)
        kp_window(w)
        if w + 1 < NT:
            b1_pass2(w + 1)
        if w + 2 < NT:
            b1_load(w + 2)
        emit_tr(4 * w)
        emit_tr(4 * w + 1)
        for j in range(4):
            tc_ = 4 * w + j
            emit_v(tc_)
            if tc_ >= 2:
                emit_kv(tc_ - 2)
            if j < 2:
                emit_tr(4 * w + 2 + j)
    emit_kv(TC - 2)
    emit_kv(TC - 1)
    for _ in gen_a1:
        pass
    wq_state['gr1'] = round_w1_gen()
    wq_state['v_fbuf'] = vcs.n
    wq_state['s_fbuf'] = scs.n
    kv_all = tns.n

    vc.wait(tns, kv_all)
    v_op("tensor_copy", kvsb[:, 0:260], pK1[:, 0:260], inc=False)
    v_op("tensor_copy", kvsb[:, 260:520], pK2[:, 0:260], inc=False)
    v_op("tensor_copy", ksum[:, 0:4],
         bass.AP(pK1h, 64, [[512, 128], [65, 4], [1, 1]]), inc=False)
    v_op("tensor_copy", ksum[:, 4:8],
         bass.AP(pK2h, 64, [[512, 128], [65, 4], [1, 1]]))
    for c in range(8):
        v_op("scalar_tensor_tensor", kvf[:, c * 64:(c + 1) * 64], bvp[:, c * 64:(c + 1) * 64],
             ksum[:, c:c + 1], kvsb[:, c * 65:c * 65 + 64], A.mult, A.add, inc=False)
    v_op("memset", scb[:, :], 0.0)
    for c in range(8):
        v_op("tensor_copy", scb[0:64, c * 16 + 2 * c:c * 16 + 2 * c + 1], ksum[0:64, c:c + 1],
             inc=False)
        v_op("tensor_copy", scb[64:128, c * 16 + 2 * c + 1:c * 16 + 2 * c + 2],
             ksum[64:128, c:c + 1], inc=(c == 7))
    v_b4 = vcs.n

    phase('B3')
    tn.wait(vcs, v_b4)
    s_qpe = {}
    v_qp = {}
    v_qkv = {}
    v_rcs = {}
    m_rc = {}
    tn_qkv = {}

    def qbuf(tt):
        return qpr if tt % 2 == 0 else ring

    def emit_qp(tt, mid=None):
        qb_ = qbuf(tt)
        for n in range(8):
            if n == 2 and mid is not None:
                mid()
            idx = tt * 8 + n
            p_ = (pN, pE, pD)[idx % 3]
            if idx >= 3:
                tn.wait(scs, s_qpe[idx - 3])
            for dc in range(8):
                mm(p_, wqr[:, dc * 1024 + n * 128:dc * 1024 + (n + 1) * 128],
                   ln1T(dc, tt * 512, (tt + 1) * 512), dc == 0, dc == 7, inc=(dc == 7))
            sc.wait(tns)
            kb = bfr[:, (idx % 2) * 1024:(idx % 2) * 1024 + 512]
            if idx >= 2:
                sc.wait(vcs, v_qp[idx - 2])
            if idx < 2:
                sc.wait(vcs, v_kp[NT * 8 - 2 + idx])
            s_qpe[idx] = s_op("activation", kb, p_, AF.Relu, bias=bqr_sb[:, n:n + 1])
            vc.wait(scs, s_qpe[idx])
            if tt >= 2:
                vc.wait(tns, tn_qkv[tt - 2])
            if tt == 1 and n == 0:
                vc.wait(tns, kv_all)
            if idx < 2:
                vc.wait(syK0)
                vc.wait(syK1)
            v_qp[idx] = v_op("tensor_scalar", qb_[:, n * 512:(n + 1) * 512], kb, EPS_AT,
                             None, A.add)
            wq_step_b3(idx)

    v_rcc = {}

    def emit_norm(tt):
        qb_ = qbuf(tt)
        tn.wait(vcs, v_qp[tt * 8 + 7])
        if tt >= 1:
            tn.wait(vcs, v_rcs[tt - 1])
        for c in range(8):
            mm(pC[0:16, :], scb[:, c * 16:(c + 1) * 16], qb_[:, c * 512:(c + 1) * 512],
               c == 0, c == 7, inc=(c == 7))
        tn_norm = tns.n
        vc.wait(tns, tn_norm)
        v_op("tensor_scalar", rcpr[0:16, :], pC[0:16, :], EPS_AT, None, A.add, inc=False)
        v_rcs[tt] = v_op("reciprocal", rcpr[0:16, :], rcpr[0:16, :])
        if tt >= 1:
            vc.wait(tns, tn_qkv[tt - 1])
        v_rcc[tt] = v_op("tensor_copy", rcb[0:16, 0:512], rcpr[0:16, :])

    def emit_qkv(tt):
        qb_ = qbuf(tt)
        tn.wait(vcs, v_rcc[tt])
        for c in range(8):
            prb = pA if c % 2 == 0 else pB
            prev = v_qkv.get((tt, c - 2)) if c >= 2 else (
                v_qkv.get((tt - 1, 6 + c)) if tt >= 1 else None)
            pq = pK1 if c % 2 == 0 else pK2
            if prev:
                tn.wait(vcs, prev)
            mm(prb, selb[0:16, c * 128:(c + 1) * 128], rcb[0:16, 0:512], True, True)
            mm(pq[0:64, :], kvf[0:64, c * 64:(c + 1) * 64], qb_[0:64, c * 512:(c + 1) * 512],
               True, True, skip=True)
            mm(pq[64:128, :], kvf[64:128, c * 64:(c + 1) * 64],
               qb_[64:128, c * 512:(c + 1) * 512],
               True, True, inc=True, tp=(64, 64), skip=True)
            tn_q = tns.n
            vc.wait(tns, tn_q)
            rbs = bfr[:, (c % 2) * 1024 + 512:(c % 2) * 1024 + 1024][:, 0:512]
            v_op("tensor_copy", rbs, prb, inc=False)
            v_qkv[(tt, c)] = v_op("tensor_tensor",
                                  a1[:, c * T + tt * 512:c * T + (tt + 1) * 512], pq, rbs,
                                  A.mult)
        tn_qkv[tt] = tns.n

    tn.wait(scs, s_kpe[NT * 8 - 1])
    tn.wait(gpe, m_setup)
    gp.wait(tns, kv_all)
    for tt in range(NT):
        emit_qp(tt, mid=(lambda t_=tt - 1: emit_norm(t_)) if tt >= 1 else None)
        for i_ in range(tt * 8 // NT, (tt + 1) * 8 // NT):
            gp.dma(gpC, wvo[:, i_ * 1024:(i_ + 1) * 1024], dwo[i_ * 128:(i_ + 1) * 128, :])
        if tt >= 1:
            emit_qkv(tt - 1)
    emit_norm(NT - 1)
    emit_qkv(NT - 1)
    gp.wait(gpC)
    gpe.n += 1
    gp.do(lambda E: E.nop().then_inc(gpe.h, 1))
    m_wo = gpe.n
    v_qkv_all = vcs.n
    for _ in gen_a2:
        pass
    for _ in wq_state['gr1']:
        pass
    wq_state['gr2'] = round_w2_gen()
    s_wq_done = scs.n

    phase('B5')
    gp.wait(tns, kv_all)
    gp.wait(scs, s_wq_done)
    gp.wait(gpH0)
    gp.wait(gpH1)
    gp.wait(gpQ0)
    gp.wait(gpQ1)
    gp.wait(gpC)
    tn.wait(gpe, m_wo)
    tn.wait(vcs, v_qkv_all)
    v_x1 = {}
    s_xsq = {}
    sy_x1 = {}
    for t in range(TC):
        xr = fbuf[:, (t % 2) * 1024:(t % 2) * 1024 + 1024]
        gx = gpA if t % 2 == 0 else gpB
        if t >= 2:
            gp.wait(vcs, v_x1[t - 2])
        g_xr = gp.dma(gx, xr, dx[t * 128:(t + 1) * 128, :])
        pz0 = pC if t % 2 == 0 else pE
        pz1 = pD if t % 2 == 0 else pN
        if t >= 2:
            tn.wait(vcs, v_x1[t - 2])
        for dc in range(8):
            stat = a1[:, dc * T + t * 128:dc * T + (t + 1) * 128]
            mm(pz0, stat, wvo[:, dc * 1024:dc * 1024 + 512], dc == 0, dc == 7)
            mm(pz1, stat, wvo[:, dc * 1024 + 512:(dc + 1) * 1024], dc == 0, dc == 7,
               inc=(dc == 7))
        vc.wait(tns)
        vc.wait(gx, g_xr)
        if t == 0:
            vc.wait(gpH0)
            vc.wait(gpH1)
        if t >= 1:
            vc.wait(scs, s_xsq[t - 1])
        v_op("scalar_tensor_tensor", zb[:, 0:512], pz0, 1.0, bo_b[:, 0:512], A.mult, A.add,
             inc=False)
        v_op("scalar_tensor_tensor", zb[:, 512:1024], pz1, 1.0, bo_b[:, 512:1024],
             A.mult, A.add)
        x1b = ob[:, (t % 2) * 1024:(t % 2) * 1024 + 1024]
        sl = syA if t % 2 == 0 else syB
        if t >= 2:
            vc.wait(sl, sy_x1[t - 2])
        v_x1[t] = v_op("scalar_tensor_tensor", x1b, zb[:, :], 1.0, xr, A.mult, A.add,
                       accum_out=x_s[:, t:t + 1])
        v_op("reduce_max", x_mx[:, t:t + 1], x1b, AX, inc=False)
        v_op("tensor_reduce", x_mn[:, t:t + 1], x1b, AX, A.min)
        sc.wait(vcs, v_x1[t])
        s_xsq[t] = s_op("activation", zb[:, :], x1b, AF.Square, accum_out=x_q[:, t:t + 1])
        sy.wait(vcs, v_x1[t])
        sy_x1[t] = sy.dma(sl, s_x1[t * 128:(t + 1) * 128, :], x1b)
        wq_step_b5()
    s_x1sq = scs.n
    tn_b5 = tns.n
    m_x1st = sy_mark([syA, syB])

    vc.wait(scs, s_x1sq)
    v_op("tensor_scalar", x_m[:, :], x_s[:, :], 1.0 / 1024, None, A.mult, inc=False)
    v_op("tensor_tensor", x_v[:, :], x_m[:, :], x_m[:, :], A.mult, inc=False)
    v_op("scalar_tensor_tensor", x_v[:, :], x_q[:, :], 1.0 / 1024, x_v[:, :], A.mult,
         A.subtract)
    sc.wait(vcs)
    s_sd2 = s_op("activation", x_v[:, :], x_v[:, :], AF.Sqrt, bias=cst[:, 0:1])
    vc.wait(scs, s_sd2)
    v_op("reciprocal", x_r[:, :], x_v[:, :], inc=False)
    v_op("scalar_tensor_tensor", x_nm[:, :], x_m[:, :], -1.0, x_r[:, :], A.mult, A.mult,
         inc=False)
    v_op("tensor_tensor", st_t1[:, 0:TC], x_mx[:, :], x_m[:, :], A.subtract, inc=False)
    v_op("tensor_tensor", st_t2[:, 0:TC], x_m[:, :], x_mn[:, :], A.subtract, inc=False)
    v_op("tensor_tensor", st_t1[:, 0:TC], st_t1[:, 0:TC], st_t2[:, 0:TC], A.max, inc=False)
    v_op("tensor_tensor", st_t1[:, 0:TC], st_t1[:, 0:TC], x_r[:, :], A.mult, inc=False)
    v_lmr = v_op("reduce_max", st_t2[:, 0:1], st_t1[:, 0:TC], AX)
    sy.wait(vcs, v_lmr)
    sy.dma(syC, g128[:, :], st_t2[:, 0:1])
    m_g1 = sy_mark([syC])
    vc.wait(sye, m_g1)
    v_lmax = v_op("reduce_max", ccs[0:1, 0:1], g128[:, :], AX)

    phase('B6')
    v_z1 = v_op("memset", ccs[0:1, 1:8], 0.0)
    gp.wait(sye, m_x1st)
    gp.wait(scs)
    gp.wait(vcs)
    gp.wait(gpH0)
    gp.wait(gpH1)
    xld = {0: gp.dma(gpA, fbuf[:, 0:1024], s_x1[0:128, :]),
           1: gp.dma(gpB, fbuf[:, 1024:2048], s_x1[128:256, :])}
    gp.wait(vcs, max(v_z1, v_lmax))
    gp.wait(gpC)
    gp.dma(gpC, cc1i.ap(), ccs[0:1, :])
    gp.wait(gpC)
    ccsm.n += 1
    gp.do(lambda E: E.collective_compute(
        "AllReduce", A.max, replica_groups=[list(range(ncores))],
        ins=[cc1i.ap().opt()], outs=[cc1o.ap().opt()]).then_inc(ccsm.h, 1))
    gp.wait(ccsm)
    gp.dma(gpC, ccs[0:1, 0:1], cc1o.ap()[0:1, 0:1])
    m_m1 = gp_mark()
    vc.wait(gpe, m_m1)
    v_op("reciprocal", sc1[0:1, 8:9], ccs[0:1, 0:1], inc=False)
    v_op("tensor_scalar", sc1[0:1, 8:9], sc1[0:1, 8:9], QP, None, A.mult, inc=False)
    v_op("tensor_scalar", sc1[0:1, 9:10], ccs[0:1, 0:1], 1.0 / QP, None, A.mult, inc=False)
    v_op("tensor_tensor", sc1[0:1, 10:11], sc1[0:1, 6:7], sc1[0:1, 9:10], A.mult)
    sy.wait(vcs)
    sy.dma(syC, s_fl[1:2, :], sc1[0:1, 8:16])
    m_fl1 = sy_mark([syC])
    gp.wait(sye, m_fl1)
    gp.dma(gpC, ppc[:, 0:1], s_fl[1:2, 0:1].partition_broadcast(128))
    gp.dma(gpC, ppc[:, 1:2], s_fl[1:2, 2:3].partition_broadcast(128))
    m_c1 = gp_mark()
    vc.wait(gpe, m_c1)
    v_op("tensor_scalar", rs2[:, :], x_r[:, :], ppc[:, 0:1], None, A.mult, inc=False)
    v_op("tensor_scalar", boff[:, :], x_nm[:, :], ppc[:, 0:1], None, A.mult, inc=False)
    vc.wait(sye, m_bias)
    v_b1 = v_op("tensor_scalar", c1b1[:, :], b1c[:, :], ppc[:, 1:2], None, A.mult)

    for _ in wq_state['gr2']:
        pass

    phase('B7')
    s_x = {}
    v_x = {}
    sy_xs = {}
    m_xw = {}
    gws = {}
    tn_w1 = {}

    def load_wslot(hc_):
        wslot_ = wvo[:, (hc_ % 2) * 1024:(hc_ % 2) * 1024 + 1024]
        ss_ = syS0 if hc_ % 2 == 0 else syS1
        if hc_ >= 2:
            sy.wait(tns, tn_w1[hc_ - 2])
        for dc_ in range(8):
            gws[hc_] = sy.dma_tr(ss_, wslot_[:, dc_ * 128:(dc_ + 1) * 128],
                                 s_w1q[hc_ * 128:(hc_ + 1) * 128,
                                       dc_ * 128:(dc_ + 1) * 128])

    sy.wait(tns, tn_b5)
    sy.wait(syW0)
    sy.wait(syW1)
    load_wslot(0)
    if HC > 1:
        load_wslot(1)
    gp.wait(sye, m_x1st)
    gp.wait(scs)
    gp.wait(vcs)
    for t in range(TC):
        xb = fbuf[:, (t % 2) * 1024:(t % 2) * 1024 + 1024]
        gx = gpA if t % 2 == 0 else gpB
        if t < 2:
            g = xld[t]
        else:
            gp.wait(scs, s_x[t - 2])
            g = gp.dma(gx, xb, s_x1[t * 128:(t + 1) * 128, :])
        sc.wait(gx, g)
        sc.wait(vcs, v_b1)
        tb = fbuf[:, 2048:3072] if t % 2 == 0 else fbuf[:, 3072:4096]
        if t >= 2:
            sc.wait(vcs, v_x[t - 2])
        s_x[t] = s_op("activation", tb, xb, AF.Identity, bias=boff[:, t:t + 1],
                      scale=rs2[:, t:t + 1])
        vc.wait(scs, s_x[t])
        qb = bfr[:, (t % 2) * 1024:(t % 2) * 1024 + 1024]
        if t == 0:
            vc.wait(syK0)
            vc.wait(syK1)
        sx = scX0 if t % 2 == 0 else scX1
        if t >= 2:
            vc.wait(sx, sy_xs[t - 2])
        v_x[t] = v_op("tensor_scalar", qb, tb, MAGIC, -MAGIC, A.add, A.add)
        sc.wait(vcs, v_x[t])
        sy_xs[t] = sc.dma(sx, s_xq[t * 128:(t + 1) * 128, :], qb)
        if t % 4 == 3:
            w = t // 4
            sy.wait(scX0)
            sy.wait(scX1)
            if w == 0:
                sy.wait(tns, tn_b5)
            for dc in range(8):
                sy.dma_tr(syC, a1[:, dc * T + w * 512:dc * T + (w + 1) * 512],
                          s_xq[w * 512:(w + 1) * 512, dc * 128:(dc + 1) * 128])
            m_xw[w] = sy_mark([syC])

    w2t = []
    for hc in range(HC):
        if hc < 8:
            w2t.append(wqr[:, hc * 1024:(hc + 1) * 1024])
        elif hc < 16:
            w2t.append(wkr[:, (hc - 8) * 1024:(hc - 8 + 1) * 1024])
        elif hc < 24:
            w2t.append(ring[:, (hc - 16) * 1024:(hc - 16 + 1) * 1024])
        elif hc < 28:
            w2t.append(qpr[:, (hc - 24) * 1024:(hc - 24 + 1) * 1024])
        else:
            w2t.append(a1[:, (hc - 28) * 1024:(hc - 28 + 1) * 1024])

    s_gel = {}
    gp_hst = {}
    v_hm = {}
    n_w2t_done = 0
    for hc in range(HC):
        wslot = wvo[:, (hc % 2) * 1024:(hc % 2) * 1024 + 1024]
        ss = syS0 if hc % 2 == 0 else syS1
        if n_w2t_done < min(28, HC):
            sy.dma_tr(syC, w2t[n_w2t_done], s_w2q[:, n_w2t_done * 128:(n_w2t_done + 1) * 128])
            n_w2t_done += 1
        tn.wait(ss, gws[hc])
        for tt in range(NT):
            idx = hc * NT + tt
            p_ = pA if idx % 2 == 0 else pB
            if hc == 0:
                tn.wait(sye, m_xw[tt])
            if idx >= 2:
                tn.wait(scs, s_gel[idx - 2])
            for dc in range(8):
                mm(p_, wslot[:, dc * 128:(dc + 1) * 128],
                   a1[:, dc * T + tt * 512:dc * T + (tt + 1) * 512],
                   dc == 0, dc == 7, inc=(dc == 7))
            sc.wait(tns)
            hb = fbuf[:, (idx % 2) * 512:(idx % 2) * 512 + 512]
            gh = gpH0 if idx % 2 == 0 else gpH1
            if idx >= 2:
                sc.wait(gh, gp_hst[idx - 2])
                sc.wait(vcs, v_hm[idx - 2])
            s_gel[idx] = s_op("activation", hb, p_, AF.Gelu, bias=c1b1[:, hc:hc + 1],
                              scale=ppc[:, 1:2])
            vc.wait(scs, s_gel[idx])
            v_hm[idx] = v_op("reduce_max", hmaxs[:, idx:idx + 1], hb, AX,
                             apply_absolute_value=True)
            gp.wait(scs, s_gel[idx])
            gp_hst[idx] = gp.dma(gh, s_h[hc * 128:(hc + 1) * 128, tt * 512:(tt + 1) * 512], hb)
        tn_w1[hc] = tns.n
        if hc + 2 < HC:
            load_wslot(hc + 2)
    while n_w2t_done < min(28, HC):
        sy.dma_tr(syC, w2t[n_w2t_done], s_w2q[:, n_w2t_done * 128:(n_w2t_done + 1) * 128])
        n_w2t_done += 1
    m_w2ta = sy_mark([syC, syS0, syS1])

    phase('B8')
    v_op("reduce_max", st_t2[:, 0:1], hmaxs[:, 0:HC * NT], AX)
    sy.wait(vcs)
    sy.dma(syC, g128[:, :], st_t2[:, 0:1])
    m_g4 = sy_mark([syC])
    vc.wait(sye, m_g4)
    v_lm2 = v_op("reduce_max", ccs[0:1, 0:1], g128[:, :], AX)
    v_z2 = v_op("memset", ccs[0:1, 1:8], 0.0)
    gp.wait(vcs, max(v_lm2, v_z2))
    gp.wait(gpH0)
    gp.wait(gpH1)
    gp.wait(scs)
    gp.wait(vcs)
    hld = {}
    hld[0] = gp.dma(gpA, fbuf[:, 0:1024],
                    bass.AP(s_hh, 0, [[T, 128], [128 * T, 8], [1, 128]]))
    _q1, _t1 = (1 % (HID // 1024), 1 // (HID // 1024)) if HID >= 1024 else (0, 1)
    hld[1] = gp.dma(gpB, fbuf[:, 1024:2048],
                    bass.AP(s_hh, (_q1 * 8) * 128 * T + _t1 * 128,
                            [[T, 128], [128 * T, 8], [1, 128]]))
    gp.wait(gpC)
    gp.dma(gpC, cc2i.ap(), ccs[0:1, :])
    gp.wait(gpC)
    ccsm.n += 1
    gp.do(lambda E: E.collective_compute(
        "AllReduce", A.max, replica_groups=[list(range(ncores))],
        ins=[cc2i.ap().opt()], outs=[cc2o.ap().opt()]).then_inc(ccsm.h, 1))
    gp.wait(ccsm)
    gp.dma(gpC, ccs[0:1, 0:1], cc2o.ap()[0:1, 0:1])
    m_m2 = gp_mark()
    vc.wait(gpe, m_m2)
    v_op("reciprocal", sc1[0:1, 8:9], ccs[0:1, 0:1], inc=False)
    v_op("tensor_scalar", sc1[0:1, 8:9], sc1[0:1, 8:9], QP, None, A.mult, inc=False)
    v_op("tensor_scalar", sc1[0:1, 9:10], ccs[0:1, 0:1], 1.0 / QP, None, A.mult, inc=False)
    v_op("tensor_tensor", sc1[0:1, 10:11], sc1[0:1, 7:8], sc1[0:1, 9:10], A.mult)
    sy.wait(vcs)
    sy.dma(syC, s_fl[2:3, :], sc1[0:1, 8:16])
    m_fl2 = sy_mark([syC])
    gp.wait(sye, m_fl2)
    gp.dma(gpC, ppc[:, 2:3], s_fl[2:3, 0:1].partition_broadcast(128))
    gp.dma(gpC, ppc[:, 3:4], s_fl[2:3, 2:3].partition_broadcast(128))
    m_c2 = gp_mark()
    vc.wait(gpe, m_c2)
    v_b2 = v_op("tensor_scalar", bo_b[:, :], b2_b[:, :], ppc[:, 3:4], None, A.mult)
    sy.wait(tns)
    for hc in range(min(28, HC), HC):
        sy.dma_tr(syC, w2t[hc], s_w2q[:, hc * 128:(hc + 1) * 128])
    m_w2t = sy_mark([syC])

    phase('B9')
    tn.wait(sye, m_w2t)
    tn.wait(sye, m_w2ta)
    tn.wait(vcs, v_b2)
    NQ = HID // 1024
    s_hq = {}
    v_hq = {}
    tn_hqu = {}
    v_out = {}
    sy_o = {}
    g_x1r = {}
    gp.wait(scs)
    gp.wait(vcs)
    gp.wait(gpH0)
    gp.wait(gpH1)
    gp.wait(gpQ0)
    gp.wait(gpQ1)
    for tt in range(TC):
        po0 = pC if tt % 2 == 0 else pE
        po1 = pD if tt % 2 == 0 else pN
        outb = ob[:, (tt % 2) * 1024:(tt % 2) * 1024 + 1024]
        gxr = gpR if tt % 2 == 0 else gpP
        sl = syA if tt % 2 == 0 else syB
        if tt >= 2:
            gp.wait(vcs, v_out[tt - 2])
            gp.wait(sl, sy_o[tt - 2])
        g_x1r[tt] = gp.dma(gxr, outb, s_x1[tt * 128:(tt + 1) * 128, :])
        for q in range(NQ):
            qidx = tt * NQ + q
            hb = fbuf[:, (qidx % 2) * 1024:(qidx % 2) * 1024 + 1024]
            gx = gpA if qidx % 2 == 0 else gpB
            if qidx < 2:
                g = hld[qidx]
            else:
                gp.wait(scs, s_hq[qidx - 2])
                g = gp.dma(gx, hb,
                           bass.AP(s_hh, (q * 8) * 128 * T + tt * 128,
                                   [[T, 128], [128 * T, 8], [1, 128]]))
            sc.wait(gx, g)
            sc.wait(gpe, m_c2)
            tb = fbuf[:, 2048:3072] if qidx % 2 == 0 else rcpb[:, 0:1024]
            if qidx >= 2:
                sc.wait(vcs, v_hq[qidx - 2])
            s_hq[qidx] = s_op("activation", tb, hb, AF.Identity, bias=cst[:, 1:2],
                              scale=ppc[:, 2:3])
            vc.wait(scs, s_hq[qidx])
            qb = bfr[:, (qidx % 2) * 1024:(qidx % 2) * 1024 + 1024]
            if qidx == 0:
                vc.wait(scX0)
                vc.wait(scX1)
            if qidx >= 2:
                vc.wait(tns, tn_hqu[qidx - 2])
            v_hq[qidx] = v_op("tensor_scalar", qb, tb, -MAGIC, None, A.add)
            if q == 0 and tt >= 2:
                tn.wait(vcs, v_out[tt - 2])
            tn.wait(vcs, v_hq[qidx])
            for hcq in range(8):
                hc = q * 8 + hcq
                first = (q == 0 and hcq == 0)
                last = (q == NQ - 1 and hcq == 7)
                mm(po0, qb[:, hcq * 128:(hcq + 1) * 128], w2t[hc][:, 0:512], first, last)
                mm(po1, qb[:, hcq * 128:(hcq + 1) * 128], w2t[hc][:, 512:1024], first, last,
                   inc=(hcq == 7))
            tn_hqu[qidx] = tns.n
        vc.wait(tns)
        vc.wait(gxr, g_x1r[tt])
        v_op("scalar_tensor_tensor", zb[:, 0:512], po0, ppc[:, 3:4], bo_b[:, 0:512],
             A.mult, A.add, inc=False)
        v_op("scalar_tensor_tensor", zb[:, 512:1024], po1, ppc[:, 3:4], bo_b[:, 512:1024],
             A.mult, A.add)
        v_out[tt] = v_op("tensor_tensor", outb, zb[:, :], outb, A.add)
        sy.wait(vcs, v_out[tt])
        sy_o[tt] = sy.dma(sl, dout[tt * 128:(tt + 1) * 128, :], outb)
    sy.wait(syA)
    sy.wait(syB)

    phase('END')
    if cut is not None:
        lens = phase_marks[cut]
        for e, L in zip((gp, sy, tn, sc, vc), lens):
            del e.ops[L:]
    with contextlib.ExitStack() as st_:
        for s, nm in ALL_SEMS:
            s.h = st_.enter_context(nc.semaphore(nm))
        with nc.Block() as block:
            @block.gpsimd
            def _(E):
                gp.run(E)

            @block.sync
            def _(E):
                sy.run(E)

            @block.tensor
            def _(E):
                tn.run(E)

            @block.scalar
            def _(E):
                sc.run(E)

            @block.vector
            def _(E):
                vc.run(E)
    return nc


def kernel(L0_attn_x, L0_attn_Wq, L0_attn_bq, L0_attn_Wk, L0_attn_bk, L0_attn_Wv, L0_attn_bv,
           L0_attn_Wo, L0_attn_bo, L0_attn_rw, L0_ffn_w1, L0_ffn_b1, L0_ffn_w2, L0_ffn_b2):
    n = 8
    x = np.ascontiguousarray(np.asarray(L0_attn_x, dtype=np.float32))
    B, S, D = x.shape
    nc = build(T=S, HID=int(L0_ffn_w1.shape[0]), ncores=n)
    com = {
        "wq": np.ascontiguousarray(L0_attn_Wq, dtype=np.float32),
        "wk": np.ascontiguousarray(L0_attn_Wk, dtype=np.float32),
        "wv": np.ascontiguousarray(L0_attn_Wv, dtype=np.float32),
        "wo": np.ascontiguousarray(L0_attn_Wo, dtype=np.float32),
        "bq": np.ascontiguousarray(L0_attn_bq, dtype=np.float32).reshape(1, -1),
        "bk": np.ascontiguousarray(L0_attn_bk, dtype=np.float32).reshape(1, -1),
        "bv": np.ascontiguousarray(L0_attn_bv, dtype=np.float32).reshape(1, -1),
        "bo": np.ascontiguousarray(L0_attn_bo, dtype=np.float32).reshape(1, -1),
        "rw": np.ascontiguousarray(L0_attn_rw, dtype=np.float32),
        "w1": np.ascontiguousarray(L0_ffn_w1, dtype=np.float32),
        "b1": np.ascontiguousarray(L0_ffn_b1, dtype=np.float32).reshape(1, -1),
        "w2": np.ascontiguousarray(L0_ffn_w2, dtype=np.float32),
        "b2": np.ascontiguousarray(L0_ffn_b2, dtype=np.float32).reshape(1, -1),
    }
    in_maps = [dict(com, x=np.ascontiguousarray(x[b])) for b in range(n)]
    res = run_bass_kernel_spmd(nc, in_maps, core_ids=list(range(n)))
    return np.stack([np.asarray(r["out"], dtype=np.float32) for r in res.results], axis=0)


# version 27
# speedup vs baseline: 1.0211x; 1.0211x over previous
import contextlib
import numpy as np
import concourse.bass as bass
import concourse.mybir as mybir
from concourse.bass_utils import run_bass_kernel_spmd

F32 = mybir.dt.float32
BF16 = mybir.dt.bfloat16
A = mybir.AluOpType
AF = mybir.ActivationFunctionType
AX = mybir.AxisListType.X
MAGIC = 12582912.0
QP = 127.0
EPS_LN = 1e-5
EPS_AT = 1e-6


class Sem:
    def __init__(self):
        self.h = None
        self.n = 0


class Eng:
    def __init__(self, name):
        self.name = name
        self.ops = []

    def do(self, fn):
        self.ops.append(fn)

    def dma(self, sem, out, in_, **kw):
        sem.n += 16
        self.ops.append(lambda E, o=out, i=in_, kw=kw, s=sem: E.dma_start(out=o, in_=i, **kw).then_inc(s.h, 16))
        return sem.n

    def dma_tr(self, sem, out, in_):
        sem.n += 16
        self.ops.append(lambda E, o=out, i=in_, s=sem: E.dma_start_transpose(o, i).then_inc(s.h, 16))
        return sem.n

    def wait(self, sem, v=None):
        v = sem.n if v is None else v
        if v > 0:
            self.ops.append(lambda E, s=sem, v=v: E.wait_ge(s.h, v))

    def run(self, E):
        for f in self.ops:
            f(E)


def build(T=4096, HID=4096, ncores=8, cut=None, dbg=False):
    TC = T // 128
    NT = T // 512
    HC = HID // 128
    D = 1024
    HCpad = ((HC + 15) // 16) * 16
    nc = bass.Bass("TRN2", target_bir_lowering=False)

    dxh = nc.dram_tensor("x", [T, D], F32, kind="ExternalInput")
    dwqh = nc.dram_tensor("wq", [D, D], F32, kind="ExternalInput")
    dwkh = nc.dram_tensor("wk", [D, D], F32, kind="ExternalInput")
    dwvh = nc.dram_tensor("wv", [D, D], F32, kind="ExternalInput")
    dwoh = nc.dram_tensor("wo", [D, D], F32, kind="ExternalInput")
    dbqh = nc.dram_tensor("bq", [1, D], F32, kind="ExternalInput")
    dbkh = nc.dram_tensor("bk", [1, D], F32, kind="ExternalInput")
    dbvh = nc.dram_tensor("bv", [1, D], F32, kind="ExternalInput")
    dboh = nc.dram_tensor("bo", [1, D], F32, kind="ExternalInput")
    drwh = nc.dram_tensor("rw", [16, 64, 64], F32, kind="ExternalInput")
    dw1h = nc.dram_tensor("w1", [HID, D], F32, kind="ExternalInput")
    db1h = nc.dram_tensor("b1", [1, HID], F32, kind="ExternalInput")
    dw2h = nc.dram_tensor("w2", [D, HID], F32, kind="ExternalInput")
    db2h = nc.dram_tensor("b2", [1, D], F32, kind="ExternalInput")
    douth = nc.dram_tensor("out", [T, D], F32, kind="ExternalOutput")
    dx, dwq, dwk, dwv, dwo = dxh.ap(), dwqh.ap(), dwkh.ap(), dwvh.ap(), dwoh.ap()
    dbq, dbk, dbv, dbo, drw = dbqh.ap(), dbkh.ap(), dbvh.ap(), dboh.ap(), drwh.ap()
    dw1, db1, dw2, db2, dout = dw1h.ap(), db1h.ap(), dw2h.ap(), db2h.ap(), douth.ap()

    KD = dict(kind="ExternalOutput") if dbg else {}
    s_wqh = nc.dram_tensor("s_wq", [D, D], BF16)
    s_wkh = nc.dram_tensor("s_wk", [D, D], BF16)
    s_smh = nc.dram_tensor("s_small", [64, 128], BF16)
    s_ln1h = nc.dram_tensor("s_ln1", [T, D], BF16, **KD)
    s_kph = nc.dram_tensor("s_kp", [D, T], BF16, **KD)
    s_qph = nc.dram_tensor("s_qp", [D, T], BF16, **KD)
    s_rch = nc.dram_tensor("s_recip", [16, T], F32, **KD)
    s_x1h = nc.dram_tensor("s_x1", [T, D], F32, **KD)
    s_xqh = nc.dram_tensor("s_xq", [T, D], BF16, **KD)
    s_w1qh = nc.dram_tensor("s_w1q", [HID, D], BF16, **KD)
    s_w1th = nc.dram_tensor("s_w1qt", [D, HID], BF16)
    s_w2qh = nc.dram_tensor("s_w2q", [D, HID], BF16, **KD)
    s_hh = nc.dram_tensor("s_h", [HID, T], F32, **KD)
    s_flh = nc.dram_tensor("s_flags", [8, 8], F32)
    d_oneh = nc.dram_tensor("d_one", [1, 64], BF16)
    cc1i = nc.dram_tensor("cc1i", [1, 8], F32)
    cc1o = nc.dram_tensor("cc1o", [1, 8], F32)
    cc2i = nc.dram_tensor("cc2i", [1, 8], F32)
    cc2o = nc.dram_tensor("cc2o", [1, 8], F32)
    s_wq, s_wk, s_sm, s_ln1 = s_wqh.ap(), s_wkh.ap(), s_smh.ap(), s_ln1h.ap()
    s_kp, s_qp, s_rc, s_x1 = s_kph.ap(), s_qph.ap(), s_rch.ap(), s_x1h.ap()
    s_xq, s_w1q, s_w1t, s_w2q = s_xqh.ap(), s_w1qh.ap(), s_w1th.ap(), s_w2qh.ap()
    s_h, s_fl = s_hh.ap(), s_flh.ap()
    d_one = d_oneh.ap()

    al = nc.alloc_sbuf_tensor
    A1C = max(8 * T, 16384)
    a1h = al("a1", [128, A1C], BF16)
    wqrh = al("wqr", [128, 8192], BF16)
    wkrh = al("wkr", [128, 8192], BF16)
    wvoh = al("wvo", [128, 8192], BF16)
    ringh = al("ring", [128, 8320], BF16)
    fbufh = al("fbuf", [128, 4096], F32)
    bfrh = al("bfr", [128, 2048], BF16)
    zbh = al("zb", [128, 1024], F32)
    obh = al("ob", [128, 2048], F32)
    a1, wqr, wkr, wvo = a1h.ap(), wqrh.ap(), wkrh.ap(), wvoh.ap()
    ring, fbuf, bfr, zb, ob = ringh.ap(), fbufh.ap(), bfrh.ap(), zbh.ap(), obh.ap()
    bo_b = al("bo_b", [128, 1024], F32).ap()
    b2_b = al("b2_b", [128, 1024], F32).ap()
    rcpr = al("rcpr", [16, 512], F32).ap()
    selb = al("selb", [16, 1024], BF16).ap()
    onesb = al("onesb", [1, 64], BF16).ap()
    rcpb = al("rcpb", [128, 1024], F32).ap()
    rwp = al("rwp", [128, 512], BF16).ap()
    bqp = al("bqp", [128, 16], BF16).ap()
    bkp = al("bkp", [128, 16], BF16).ap()
    b1c = al("b1c", [128, HCpad], BF16).ap()
    bvp = al("bvp", [128, 512], BF16).ap()
    kvsb = al("kvsb", [128, 520], BF16).ap()
    rcb = kvsb
    kvf = al("kvf", [128, 512], BF16).ap()
    ksum = al("ksum", [128, 8], F32).ap()
    scb = al("scb", [128, 128], BF16).ap()
    qpr = al("qpr", [128, 4096], BF16).ap()
    st_s = al("st_s", [128, TC], F32).ap()
    st_q = al("st_q", [128, TC], F32).ap()
    st_m = al("st_m", [128, TC], F32).ap()
    st_r = al("st_r", [128, TC], F32).ap()
    st_nm = al("st_nm", [128, TC], F32).ap()
    st_t1 = al("st_t1", [128, TC], F32).ap()
    st_t2 = al("st_t2", [128, TC], F32).ap()
    x_s = al("x_s", [128, TC], F32).ap()
    x_q = al("x_q", [128, TC], F32).ap()
    x_mx = al("x_mx", [128, TC], F32).ap()
    x_mn = al("x_mn", [128, TC], F32).ap()
    x_m = al("x_m", [128, TC], F32).ap()
    x_r = al("x_r", [128, TC], F32).ap()
    x_v = al("x_v", [128, TC], F32).ap()
    x_nm = al("x_nm", [128, TC], F32).ap()
    rs2 = al("rs2", [128, TC], F32).ap()
    boff = al("boff", [128, TC], F32).ap()
    c1b1 = al("c1b1", [128, HCpad], F32).ap()
    wmax = al("wmax", [128, 32], F32).ap()
    hmaxs = al("hmaxs", [128, max(HC * NT, 8)], F32).ap()
    g128 = al("g128", [1, 128], F32).ap()
    sc1 = al("sc1", [1, 16], F32).ap()
    ccs = al("ccs", [1, 8], F32).ap()
    ppc = al("ppc", [128, 8], F32).ap()
    zrow = al("zrow", [1, 1024], BF16).ap()
    brow = al("brow", [1, max(HID, 1024)], BF16).ap()
    wqb = al("wqb", [128, 2048], BF16).ap()
    cst = al("cst", [128, 2], F32).ap()
    bqr_sb = al("bqr_sb", [128, 8], F32).ap()
    bkr_sb = al("bkr_sb", [128, 8], F32).ap()

    ap_ = nc.alloc_psum_tensor
    pA = ap_("pA", [128, 512], F32).ap()
    pB = ap_("pB", [128, 512], F32).ap()
    pC = ap_("pC", [128, 512], F32).ap()
    pD = ap_("pD", [128, 512], F32).ap()
    pK1h = ap_("pK1", [128, 512], F32)
    pK2h = ap_("pK2", [128, 512], F32)
    pK1, pK2 = pK1h.ap(), pK2h.ap()
    pN = ap_("pN", [128, 512], F32).ap()
    pE = ap_("pE", [128, 512], F32).ap()

    gp, sy, tn, sc, vc = Eng("gp"), Eng("sy"), Eng("tn"), Eng("sc"), Eng("vc")
    phase_marks = {}
    def phase(name):
        if name not in phase_marks:
            phase_marks[name] = tuple(len(e.ops) for e in (gp, sy, tn, sc, vc))
    tns, scs, vcs, ccsm = Sem(), Sem(), Sem(), Sem()
    gpe, sye = Sem(), Sem()
    gpA, gpB, gpQ0, gpQ1, gpH0, gpH1, gpR, gpP, gpP2a, gpP2b, gpC = (
        Sem(), Sem(), Sem(), Sem(), Sem(), Sem(), Sem(), Sem(), Sem(), Sem(), Sem())
    scL0, scL1, scH, scX0, scX1, scE = Sem(), Sem(), Sem(), Sem(), Sem(), Sem()
    syA, syB, syK0, syK1, syX0, syX1, syW0, syW1, syS0, syS1, syC = (
        Sem(), Sem(), Sem(), Sem(), Sem(), Sem(), Sem(), Sem(), Sem(), Sem(), Sem())
    syT = [Sem(), Sem(), Sem(), Sem()]
    ALL_SEMS = [(tns, "tns"), (scs, "scs"), (vcs, "vcs"), (ccsm, "ccsm"),
                (scL0, "scL0"), (scL1, "scL1"), (scH, "scH"), (scX0, "scX0"), (scX1, "scX1"), (scE, "scE"),
                (gpe, "gpe"), (sye, "sye"),
                (gpA, "gpA"), (gpB, "gpB"), (gpQ0, "gpQ0"), (gpQ1, "gpQ1"),
                (gpH0, "gpH0"), (gpH1, "gpH1"), (gpR, "gpR"), (gpP, "gpP"),
                (gpP2a, "gpP2a"), (gpP2b, "gpP2b"), (gpC, "gpC"),
                (syA, "syA"), (syB, "syB"), (syK0, "syK0"), (syK1, "syK1"),
                (syX0, "syX0"), (syX1, "syX1"), (syW0, "syW0"), (syW1, "syW1"),
                (syS0, "syS0"), (syS1, "syS1"), (syC, "syC"),
                (syT[0], "syT0"), (syT[1], "syT1"), (syT[2], "syT2"), (syT[3], "syT3")]

    def gp_mark():
        for s_ in (gpA, gpB, gpQ0, gpQ1, gpH0, gpH1, gpR, gpP, gpP2a, gpP2b, gpC):
            gp.wait(s_)
        gpe.n += 1
        gp.do(lambda E, v=gpe.n: E.nop().then_inc(gpe.h, 1))
        return gpe.n

    def sy_mark(sems=None):
        for s_ in (sems if sems is not None else
                   (syA, syB, syK0, syK1, syX0, syX1, syW0, syW1, syS0, syS1, syC,
                    syT[0], syT[1], syT[2], syT[3])):
            sy.wait(s_)
        sye.n += 1
        sy.do(lambda E, v=sye.n: E.nop().then_inc(sye.h, 1))
        return sye.n

    def mm(p_, lhsT, rhs, start, stop, inc=False, tp=None, skip=False):
        if inc:
            tns.n += 1
        def f(E, p_=p_, l=lhsT, r=rhs, st=start, sp=stop, tp=tp, inc=inc, sk=skip):
            i = E.matmul(p_, l, r, start=st, stop=sp, tile_position=tp, skip_group_check=sk)
            if inc:
                i.then_inc(tns.h, 1)
        tn.ops.append(f)
        return tns.n

    def v_op(name, *args, inc=True, **kw):
        prev = vcs.n
        vcs.n += 1
        def f(E, name=name, args=args, kw=kw, prev=prev):
            if prev > 0:
                E.wait_ge(vcs.h, prev)
            i = getattr(E, name)(*args, **kw)
            i.then_inc(vcs.h, 1)
        vc.ops.append(f)
        return vcs.n

    def s_op(name, *args, inc=True, **kw):
        prev = scs.n
        scs.n += 1
        def f(E, name=name, args=args, kw=kw, prev=prev):
            if prev > 0:
                E.wait_ge(scs.h, prev)
            i = getattr(E, name)(*args, **kw)
            i.then_inc(scs.h, 1)
        sc.ops.append(f)
        return scs.n


    v_op("memset", cst[:, 0:1], EPS_LN, inc=False)
    v_op("memset", sc1[:, :], 0.0, inc=False)
    v_op("memset", ccs[:, :], 0.0, inc=False)
    v_op("memset", selb[:, :], 0.0, inc=False)
    v_op("memset", onesb[0:1, :], 1.0, inc=False)
    v_cst = v_op("memset", cst[:, 1:2], MAGIC)
    gp.dma(gpC, rwp[0:64, :], bass.AP(drwh, 0, [[64, 64], [8192, 8], [1, 64]]))
    gp.dma(gpC, rwp[64:128, :], bass.AP(drwh, 4096, [[64, 64], [8192, 8], [1, 64]]))
    gpe.n += 1
    gp.do(lambda E: E.memset(zrow[:, :], 0.0).then_inc(gpe.h, 1))
    gpe.n += 1
    gp.do(lambda E: E.memset(brow[:, :], 0.0).then_inc(gpe.h, 1))
    gp.wait(gpe, gpe.n)

    def stage_bias(dsrc, n_elem, row0):
        gp.wait(gpC)
        gp.dma(gpC, brow[0:1, 0:n_elem], dsrc[0:1, 0:n_elem])
        gp.wait(gpC)
        nrow = n_elem // 128
        gp.dma(gpC, s_sm[row0:row0 + nrow, :], brow[0:1, 0:n_elem])
        pad = (-nrow) % 16
        if pad:
            gp.dma(gpC, s_sm[row0 + nrow:row0 + nrow + pad, :], zrow[0:1, 0:pad * 128])

    phase('P0a')
    stage_bias(dbq, 1024, 0)
    stage_bias(dbk, 1024, 16)
    stage_bias(db1, HID, 32)
    gp.dma(gpC, bo_b[:, :], dbo[0:1, :].partition_broadcast(128))
    gp.dma(gpC, b2_b[:, :], db2[0:1, :].partition_broadcast(128))
    gp.dma(gpC, bvp[0:64, :], bass.AP(dbvh, 0, [[0, 64], [128, 8], [1, 64]]))
    gp.dma(gpC, bvp[64:128, :], bass.AP(dbvh, 64, [[0, 64], [128, 8], [1, 64]]))
    gp.wait(vcs, v_cst)
    gp.dma(gpC, d_one[0:1, :], onesb[0:1, :])
    gp.wait(gpC)
    for c_ in range(8):
        gp.dma(gpC, selb[2 * c_:2 * c_ + 1, c_ * 128:c_ * 128 + 64], d_one[0:1, :])
        gp.dma(gpC, selb[2 * c_ + 1:2 * c_ + 2, c_ * 128 + 64:c_ * 128 + 128],
               d_one[0:1, :])
    m_setup = gp_mark()

    phase('P0b')
    sy.wait(gpe, m_setup)
    sy.dma_tr(syC, bqp[:, :], s_sm[0:16, :])
    sy.dma_tr(syC, bkp[:, :], s_sm[16:32, :])
    sy.dma_tr(syC, b1c[:, :], s_sm[32:32 + HCpad, :])
    m_bias = sy_mark([syC])

    def stage_w(dsrc, sdst):
        gp.wait(gpC)
        phase('Pc2')
        for i in range(8):
            gp.dma(gpC, a1[:, i * 1024:(i + 1) * 1024], dsrc[i * 128:(i + 1) * 128, :])
        phase('Pc3')
        gp.wait(gpC)
        for i in range(8):
            gp.dma(gpC, sdst[i * 128:(i + 1) * 128, :], a1[:, i * 1024:(i + 1) * 1024])
        phase('Pc4')
        return gp_mark()

    def prep_wr(m_stage, sdst, wrdst, bias_pair, bqr_dst):
        phase('Pd1')
        sy.wait(gpe, m_stage)
        sy.wait(tns)
        for ec in range(8):
            sy.dma_tr(syC, a1[:, 8192 + ec * 1024:8192 + (ec + 1) * 1024],
                      sdst[:, ec * 128:(ec + 1) * 128])
        m_wt = sy_mark([syC])
        phase('Pd2')
        tn.wait(sye, m_wt)
        tn.wait(gpe, m_setup)
        wrh = wrdst.copy() if False else wrdst
        for dc in range(8):
            tn.wait(scs)
            for c in range(8):
                base = 8192 + c * 1024 + dc * 128
                mm(pA[:, c * 64:(c + 1) * 64], a1[0:64, base:base + 128],
                   rwp[0:64, c * 64:(c + 1) * 64], True, True)
            for c in range(8):
                base = 8192 + c * 1024 + dc * 128
                mm(pB[:, c * 64:(c + 1) * 64], a1[64:128, base:base + 128],
                   rwp[64:128, c * 64:(c + 1) * 64], True, True, inc=(c == 7))
            sc.wait(tns)
            dste = bass.AP(wrdst.tensor, wrdst.offset + dc * 1024,
                           [[wrdst.ap[0][0], 128], [128, 8], [1, 64]])
            dsto = bass.AP(wrdst.tensor, wrdst.offset + dc * 1024 + 64,
                           [[wrdst.ap[0][0], 128], [128, 8], [1, 64]])
            s_op("copy", dste, pA)
            s_op("copy", dsto, pB)
        phase('Pd3')
        tn.wait(vcs)
        tn.wait(sye, m_bias)
        for c in range(8):
            mm(pN[0:64, c:c + 1], rwp[0:64, c * 64:(c + 1) * 64], bias_pair[0:64, c:c + 1],
               c == 0, c == 7, skip=True)
            mm(pN[64:128, c:c + 1], rwp[64:128, c * 64:(c + 1) * 64], bias_pair[64:128, c:c + 1],
               c == 0, c == 7, inc=(c == 7), tp=(64, 64), skip=True)
        vc.wait(tns)
        v_op("tensor_copy", bqr_dst[:, :], pN[:, 0:8])

    phase('P0c')
    sc.wait(vcs, v_cst)
    phase('P0c5')
    m_wq = stage_w(dwq, s_wq)
    prep_wr(m_wq, s_wq, wqr, bqp, bqr_sb)
    phase('P0d')
    m_wk = stage_w(dwk, s_wk)
    prep_wr(m_wk, s_wk, wkr, bkp, bkr_sb)
    gp.wait(gpC)
    for i in range(8):
        gp.dma(gpC, wvo[:, i * 1024:(i + 1) * 1024], dwv[i * 128:(i + 1) * 128, :])
    m_wv = gp_mark()
    tn_prep = tns.n
    s_prep = scs.n

    NW1 = HID // 128
    NW2q = (D // 128) * (HID // 1024)

    NW1 = HID // 128
    NW2q = (D // 128) * (HID // 1024)

    wq_state = {}

    def absmax_w1_gen():
        slots = [ob[:, 0:1024], ob[:, 1024:2048]]
        ssem = [gpH0, gpH1]
        v_wm = {}
        for i in range(NW1):
            wb = slots[i % 2]
            gx = ssem[i % 2]
            if i >= 2:
                gp.wait(vcs, v_wm[i - 2])
            g = gp.dma(gx, wb, dw1[i * 128:(i + 1) * 128, :])
            vc.wait(gx, g)
            v_wm[i] = v_op("reduce_max", wmax[:, i:i + 1], wb, AX, apply_absolute_value=True)
            yield
        v_op("reduce_max", wmax[:, 0:1], wmax[:, 0:NW1], AX)
        sy.wait(vcs)
        sy.dma(syC, g128[:, :], wmax[:, 0:1])
        m_g2 = sy_mark([syC])
        vc.wait(sye, m_g2)
        v_op("reduce_max", sc1[0:1, 0:1], g128[:, :], AX, inc=False)
        v_op("reciprocal", sc1[0:1, 2:3], sc1[0:1, 0:1], inc=False)
        v_op("tensor_scalar", sc1[0:1, 4:5], sc1[0:1, 2:3], QP, None, A.mult, inc=False)
        v_op("tensor_scalar", sc1[0:1, 6:7], sc1[0:1, 0:1], 1.0 / QP, None, A.mult)
        sy.wait(vcs)
        sy.dma(syC, s_fl[0:1, :], sc1[0:1, 0:8])
        m_fl0 = sy_mark([syC])
        gp.wait(sye, m_fl0)
        gp.wait(gpC)
        gp.dma(gpC, ppc[:, 4:5], s_fl[0:1, 4:5].partition_broadcast(128))
        wq_state['m_ppc1'] = gp_mark()
        wq_state['aw1_v'] = vcs.n

    def absmax_w2_gen():
        slots = [ob[:, 0:1024], ob[:, 1024:2048]]
        ssem = [gpQ0, gpQ1]
        v_wm2 = {}
        idx = 0
        for i in range(D // 128):
            for quar in range(HID // 1024):
                wb = slots[idx % 2]
                gx = ssem[idx % 2]
                if idx >= 2:
                    gp.wait(vcs, v_wm2[idx - 2])
                else:
                    gp.wait(vcs, wq_state['aw1_v'])
                g = gp.dma(gx, wb, dw2[i * 128:(i + 1) * 128, quar * 1024:(quar + 1) * 1024])
                vc.wait(gx, g)
                v_wm2[idx] = v_op("reduce_max", wmax[:, idx:idx + 1], wb, AX,
                                  apply_absolute_value=True)
                idx += 1
                yield
        v_op("reduce_max", st_t2[:, 1:2], wmax[:, 0:NW2q], AX)
        sy.wait(vcs)
        sy.dma(syC, g128[:, :], st_t2[:, 1:2])
        m_g3 = sy_mark([syC])
        vc.wait(sye, m_g3)
        v_op("reduce_max", sc1[0:1, 1:2], g128[:, :], AX, inc=False)
        v_op("reciprocal", sc1[0:1, 3:4], sc1[0:1, 1:2], inc=False)
        v_op("tensor_scalar", sc1[0:1, 5:6], sc1[0:1, 3:4], QP, None, A.mult, inc=False)
        v_op("tensor_scalar", sc1[0:1, 7:8], sc1[0:1, 1:2], 1.0 / QP, None, A.mult)
        sy.wait(vcs)
        sy.dma(syC, s_fl[3:4, :], sc1[0:1, 0:8])
        m_fl3 = sy_mark([syC])
        gp.wait(sye, m_fl3)
        gp.wait(gpC)
        gp.dma(gpC, ppc[:, 5:6], s_fl[3:4, 5:6].partition_broadcast(128))
        wq_state['m_ppc2'] = gp_mark()
        wq_state['aw2_v'] = vcs.n

    def round_w1_gen():
        m_ppc = wq_state['m_ppc1']
        s_w1 = {}
        v_w1 = {}
        sy_w1 = {}
        for i in range(NW1):
            wb = fbuf[:, (i % 2) * 1024:(i % 2) * 1024 + 1024]
            gx = gpH0 if i % 2 == 0 else gpH1
            if i == 0:
                gp.wait(vcs, wq_state['v_fbuf'])
                gp.wait(scs, wq_state['s_fbuf'])
                gp.wait(gpA)
                gp.wait(gpB)
                gp.wait(gpQ0)
                gp.wait(gpQ1)
            if i >= 2:
                gp.wait(scs, s_w1[i - 2])
            g = gp.dma(gx, wb, dw1[i * 128:(i + 1) * 128, :])
            sc.wait(gx, g)
            sc.wait(gpe, m_ppc)
            tb = fbuf[:, 2048 + (i % 2) * 1024:2048 + (i % 2) * 1024 + 1024]
            if i >= 2:
                sc.wait(vcs, v_w1[i - 2])
            s_w1[i] = s_op("activation", tb, wb, AF.Identity, bias=cst[:, 1:2],
                           scale=ppc[:, 4:5])
            vc.wait(scs, s_w1[i])
            qb = wqb[:, (i % 2) * 1024:(i % 2) * 1024 + 1024]
            sw = syW0 if i % 2 == 0 else syW1
            if i >= 2:
                vc.wait(sw, sy_w1[i - 2])
            v_w1[i] = v_op("tensor_scalar", qb, tb, -MAGIC, None, A.add)
            sy.wait(vcs, v_w1[i])
            sy_w1[i] = sy.dma(sw, s_w1q[i * 128:(i + 1) * 128, :], qb)
            yield
        wq_state['w1_s'] = scs.n
        wq_state['w1_v'] = vcs.n
        wq_state['w1_sy'] = (syW0.n, syW1.n)

    def round_w2_gen():
        m_ppc = wq_state['m_ppc2']
        s_w2 = {}
        v_w2 = {}
        sy_w2 = {}
        idx = 0
        for i in range(D // 128):
            for quar in range(HID // 1024):
                wb = fbuf[:, 2048 + (idx % 2) * 1024:2048 + (idx % 2) * 1024 + 1024]
                if idx == 0:
                    sc.wait(vcs, wq_state['w1_v'])
                    sc.wait(scs, wq_state['w1_s'])
                    sc.wait(gpH0)
                    sc.wait(gpH1)
                if idx >= 2:
                    sc.wait(scs, s_w2[idx - 2])
                g = sc.dma(scH, wb, dw2[i * 128:(i + 1) * 128, quar * 1024:(quar + 1) * 1024])
                sc.wait(scH, g)
                sc.wait(gpe, m_ppc)
                tb = rcpb[:, 0:1024]
                if idx >= 1:
                    sc.wait(vcs, v_w2[idx - 1])
                s_w2[idx] = s_op("activation", tb, wb, AF.Identity, bias=cst[:, 1:2],
                                 scale=ppc[:, 5:6])
                vc.wait(scs, s_w2[idx])
                qb = wqb[:, (idx % 2) * 1024:(idx % 2) * 1024 + 1024]
                sw = syW0 if idx % 2 == 0 else syW1
                if idx == 0:
                    vc.wait(syW0, wq_state['w1_sy'][0])
                    vc.wait(syW1, wq_state['w1_sy'][1])
                if idx >= 2:
                    vc.wait(sw, sy_w2[idx - 2])
                v_w2[idx] = v_op("tensor_scalar", qb, tb, -MAGIC, None, A.add)
                sy.wait(vcs, v_w2[idx])
                sy_w2[idx] = sy.dma(sw, s_w2q[i * 128:(i + 1) * 128,
                                              quar * 1024:(quar + 1) * 1024], qb)
                idx += 1
                yield

    gen_a1 = absmax_w1_gen()
    gen_a2 = absmax_w2_gen()
    gen_r1 = None
    gen_r2 = None

    def wq_step():
        try:
            next(gen_a1)
        except StopIteration:
            pass

    def wq_step_b3(i):
        g1 = gen_a2 if i % 2 == 0 else wq_state['gr1']
        g2 = wq_state['gr1'] if i % 2 == 0 else gen_a2
        try:
            next(g1)
            return
        except StopIteration:
            pass
        try:
            next(g2)
        except StopIteration:
            pass

    def wq_step_b5():
        try:
            next(wq_state['gr2'])
        except StopIteration:
            pass

    phase('B1')
    for j in range(4):
        off = 4096 + j * 1040 + 64
        v_op("memset", bass.AP(ringh, off, [[8320, 128], [65, 16], [1, 1]]), 1.0,
             inc=(j == 3))
    v_ones = vcs.n

    v_r1 = {}
    s_sq = {}
    v_ln = {}
    s_ln = {}
    m_w = {}
    s_kpe = {}
    v_kp = {}
    sy_kps = {}
    kv_done = {}
    kst0 = {}
    kst1 = {}

    g_x = {}

    def b1_load(w):
        for t in range(4 * w, 4 * w + 4):
            xb = fbuf[:, (t % 4) * 1024:(t % 4) * 1024 + 1024]
            gx = (gpA, gpB, gpQ0, gpQ1)[t % 4]
            if t >= 4:
                gp.wait(vcs, v_ln[t - 4])
                gp.wait(scs, s_sq[t - 4])
            g_x[t] = gp.dma(gx, xb, dx[t * 128:(t + 1) * 128, :])

    def b1_pass1(w):
        for t in range(4 * w, 4 * w + 4):
            xb = fbuf[:, (t % 4) * 1024:(t % 4) * 1024 + 1024]
            gx = (gpA, gpB, gpQ0, gpQ1)[t % 4]
            vc.wait(gx, g_x[t])
            v_r1[t] = v_op("reduce_sum", st_s[:, t:t + 1], xb, AX)
            sc.wait(gx, g_x[t])
            s_sq[t] = s_op("activation", zb[:, :], xb, AF.Square,
                           accum_out=st_q[:, t:t + 1])

    def b1_pass2(w):
        c0, c1_ = 4 * w, 4 * w + 4
        vc.wait(scs, s_sq[4 * w + 3])
        v_op("tensor_scalar", st_m[:, c0:c1_], st_s[:, c0:c1_], 1.0 / 1024, None, A.mult,
             inc=False)
        v_op("tensor_tensor", st_t2[:, c0:c1_], st_m[:, c0:c1_], st_m[:, c0:c1_], A.mult,
             inc=False)
        v_st = v_op("scalar_tensor_tensor", st_t1[:, c0:c1_], st_q[:, c0:c1_], 1.0 / 1024,
                    st_t2[:, c0:c1_], A.mult, A.subtract)
        sc.wait(vcs, v_st)
        s_sd = s_op("activation", st_t2[:, c0:c1_], st_t1[:, c0:c1_], AF.Sqrt,
                    bias=cst[:, 0:1])
        vc.wait(scs, s_sd)
        v_op("reciprocal", st_r[:, c0:c1_], st_t2[:, c0:c1_], inc=False)
        v_op("scalar_tensor_tensor", st_nm[:, c0:c1_], st_m[:, c0:c1_], -1.0,
             st_r[:, c0:c1_], A.mult, A.mult)
        for t in range(4 * w, 4 * w + 4):
            xb = fbuf[:, (t % 4) * 1024:(t % 4) * 1024 + 1024]
            lb = wqb[:, (t % 2) * 1024:(t % 2) * 1024 + 1024]
            sl = scL0 if t % 2 == 0 else scL1
            if t >= 2:
                vc.wait(sl, s_ln[t - 2])
            v_ln[t] = v_op("tensor_scalar", lb, xb, st_r[:, t:t + 1], st_nm[:, t:t + 1],
                           A.mult, A.add)
            sc.wait(vcs, v_ln[t])
            s_ln[t] = sc.dma(sl, s_ln1[t * 128:(t + 1) * 128, :], lb)
        sc.wait(scL0)
        sc.wait(scL1)
        sc.wait(scE)
        if w == 0:
            sc.wait(tns, tn_prep)
        for dc in range(8):
            m_w[w] = sc.dma_tr(scE, a1[:, dc * T + w * 512:dc * T + (w + 1) * 512],
                               s_ln1[w * 512:(w + 1) * 512, dc * 128:(dc + 1) * 128])

    def ln1T(dc, c0, c1_):
        return a1[:, dc * T + c0:dc * T + c1_]

    def kp_window(tt):
        tn.wait(scE, m_w[tt])
        if tt == 0:
            tn.wait(scs, s_prep)
        else:
            tn.wait(vcs, v_vev[4 * tt - 1])
        for n in range(8):
            idx = tt * 8 + n
            p_ = (pA, pB, pC, pD)[idx % 4]
            if idx >= 4:
                tn.wait(scs, s_kpe[idx - 4])
            for dc in range(8):
                mm(p_, wkr[:, dc * 1024 + n * 128:dc * 1024 + (n + 1) * 128],
                   ln1T(dc, tt * 512, (tt + 1) * 512), dc == 0, dc == 7, inc=(dc == 7))
            sc.wait(tns)
            kb = bfr[:, (idx % 2) * 1024:(idx % 2) * 1024 + 512]
            ke = bfr[:, (idx % 2) * 1024 + 512:(idx % 2) * 1024 + 1024]
            if idx >= 2:
                sc.wait(vcs, v_kp[idx - 2])
            s_kpe[idx] = s_op("activation", kb, p_, AF.Relu, bias=bkr_sb[:, n:n + 1])
            vc.wait(scs, s_kpe[idx])
            sk = syK0 if idx % 2 == 0 else syK1
            if idx >= 2:
                vc.wait(sk, sy_kps[idx - 2])
            v_kp[idx] = v_op("tensor_scalar", ke, kb, EPS_AT, None, A.add)
            sy.wait(vcs, v_kp[idx])
            sy_kps[idx] = sy.dma(sk, s_kp[n * 128:(n + 1) * 128, tt * 512:(tt + 1) * 512],
                                 ke)
            wq_step()
        kst0[tt] = syK0.n
        kst1[tt] = syK1.n

    kv_first = [[True, True], [True, True]]
    v_vev = {}

    def emit_v(tc_):
        slot = tc_ % 4
        vs0 = 4096 + slot * 1040
        pv0 = (pC, pE, pA)[tc_ % 3]
        pv1 = (pD, pN, pB)[tc_ % 3]
        if tc_ == 0:
            tn.wait(gpe, m_wv)
        if tc_ % 4 == 0:
            tn.wait(scs, s_kpe[(tc_ // 4) * 8 + 7])
        if tc_ >= 3:
            tn.wait(vcs, v_vev[tc_ - 3])
        for dc in range(8):
            stat = ln1T(dc, tc_ * 128, (tc_ + 1) * 128)
            mm(pv0, stat, wvo[:, dc * 1024:dc * 1024 + 512], dc == 0, dc == 7)
            mm(pv1, stat, wvo[:, dc * 1024 + 512:(dc + 1) * 1024], dc == 0, dc == 7,
               inc=(dc == 7))
        vc.wait(tns)
        vc.wait(vcs, v_ones)
        if tc_ >= 4:
            vc.wait(tns, kv_done[tc_ - 4])
        v_op("tensor_copy", bass.AP(ringh, vs0, [[8320, 128], [65, 8], [1, 64]]), pv0,
             inc=False)
        v_vev[tc_] = v_op("tensor_copy",
                          bass.AP(ringh, vs0 + 8 * 65, [[8320, 128], [65, 8], [1, 64]]), pv1)

    def emit_kv(tc_):
        slot = tc_ % 4
        tn.wait(vcs, v_vev[tc_])
        tn.wait(syT[slot])
        for c in range(8):
            bk_ = 0 if c < 4 else 1
            pk = pK1 if c < 4 else pK2
            off = (c % 4) * 65
            st0 = kv_first[bk_][0]
            st1 = kv_first[bk_][1]
            kv_first[bk_] = [False, False]
            last = (tc_ == TC - 1) and (c % 4 == 3)
            kslot = ring[:, slot * 1024:(slot + 1) * 1024]
            vs = ring[:, 4096 + slot * 1040:4096 + (slot + 1) * 1040]
            mm(pk[0:64, off:off + 65], kslot[:, (2 * c) * 64:(2 * c + 1) * 64],
               vs[:, (2 * c) * 65:(2 * c + 1) * 65], st0, last, skip=True)
            mm(pk[64:128, off:off + 65], kslot[:, (2 * c + 1) * 64:(2 * c + 2) * 64],
               vs[:, (2 * c + 1) * 65:(2 * c + 2) * 65], st1, last,
               inc=(c == 7), tp=(0, 64), skip=True)
        kv_done[tc_] = tns.n

    def emit_tr(tc_):
        slot = tc_ % 4
        ttk = tc_ // 4
        sy.wait(syK0, kst0[ttk])
        sy.wait(syK1, kst1[ttk])
        if tc_ >= 4:
            sy.wait(tns, kv_done[tc_ - 4])
        sy.dma_tr(syT[slot], ring[:, slot * 1024:(slot + 1) * 1024],
                  s_kp[:, tc_ * 128:(tc_ + 1) * 128])

    phase('B2')
    b1_load(0)
    b1_pass1(0)
    b1_pass2(0)
    if NT > 1:
        b1_load(1)
    for w in range(NT):
        if w + 1 < NT:
            b1_pass1(w + 1)
        kp_window(w)
        if w + 1 < NT:
            b1_pass2(w + 1)
        if w + 2 < NT:
            b1_load(w + 2)
        emit_tr(4 * w)
        emit_tr(4 * w + 1)
        for j in range(4):
            tc_ = 4 * w + j
            emit_v(tc_)
            if tc_ >= 2:
                emit_kv(tc_ - 2)
            if j < 2:
                emit_tr(4 * w + 2 + j)
    emit_kv(TC - 2)
    emit_kv(TC - 1)
    for _ in gen_a1:
        pass
    wq_state['gr1'] = round_w1_gen()
    wq_state['v_fbuf'] = vcs.n
    wq_state['s_fbuf'] = scs.n
    kv_all = tns.n

    vc.wait(tns, kv_all)
    v_op("tensor_copy", kvsb[:, 0:260], pK1[:, 0:260], inc=False)
    v_op("tensor_copy", kvsb[:, 260:520], pK2[:, 0:260], inc=False)
    v_op("tensor_copy", ksum[:, 0:4],
         bass.AP(pK1h, 64, [[512, 128], [65, 4], [1, 1]]), inc=False)
    v_op("tensor_copy", ksum[:, 4:8],
         bass.AP(pK2h, 64, [[512, 128], [65, 4], [1, 1]]))
    for c in range(8):
        v_op("scalar_tensor_tensor", kvf[:, c * 64:(c + 1) * 64], bvp[:, c * 64:(c + 1) * 64],
             ksum[:, c:c + 1], kvsb[:, c * 65:c * 65 + 64], A.mult, A.add, inc=False)
    v_op("memset", scb[:, :], 0.0)
    for c in range(8):
        v_op("tensor_copy", scb[0:64, c * 16 + 2 * c:c * 16 + 2 * c + 1], ksum[0:64, c:c + 1],
             inc=False)
        v_op("tensor_copy", scb[64:128, c * 16 + 2 * c + 1:c * 16 + 2 * c + 2],
             ksum[64:128, c:c + 1], inc=(c == 7))
    v_b4 = vcs.n

    phase('B3')
    tn.wait(vcs, v_b4)
    s_qpe = {}
    v_qp = {}
    v_qkv = {}
    v_rcs = {}
    m_rc = {}
    tn_qkv = {}

    def qbuf(tt):
        return qpr if tt % 2 == 0 else ring

    def emit_qp(tt, mid=None):
        qb_ = qbuf(tt)
        for n in range(8):
            if n == 2 and mid is not None:
                mid()
            idx = tt * 8 + n
            p_ = (pN, pE, pD)[idx % 3]
            if idx >= 3:
                tn.wait(scs, s_qpe[idx - 3])
            for dc in range(8):
                mm(p_, wqr[:, dc * 1024 + n * 128:dc * 1024 + (n + 1) * 128],
                   ln1T(dc, tt * 512, (tt + 1) * 512), dc == 0, dc == 7, inc=(dc == 7))
            sc.wait(tns)
            kb = bfr[:, (idx % 2) * 1024:(idx % 2) * 1024 + 512]
            if idx >= 2:
                sc.wait(vcs, v_qp[idx - 2])
            if idx < 2:
                sc.wait(vcs, v_kp[NT * 8 - 2 + idx])
            s_qpe[idx] = s_op("activation", kb, p_, AF.Relu, bias=bqr_sb[:, n:n + 1])
            vc.wait(scs, s_qpe[idx])
            if tt >= 2:
                vc.wait(tns, tn_qkv[tt - 2])
            if tt == 1 and n == 0:
                vc.wait(tns, kv_all)
            if idx < 2:
                vc.wait(syK0)
                vc.wait(syK1)
            v_qp[idx] = v_op("tensor_scalar", qb_[:, n * 512:(n + 1) * 512], kb, EPS_AT,
                             None, A.add)
            wq_step_b3(idx)

    v_rcc = {}

    def emit_norm(tt):
        qb_ = qbuf(tt)
        tn.wait(vcs, v_qp[tt * 8 + 7])
        if tt >= 1:
            tn.wait(vcs, v_rcs[tt - 1])
        for c in range(8):
            mm(pC[0:16, :], scb[:, c * 16:(c + 1) * 16], qb_[:, c * 512:(c + 1) * 512],
               c == 0, c == 7, inc=(c == 7))
        tn_norm = tns.n
        vc.wait(tns, tn_norm)
        v_op("tensor_scalar", rcpr[0:16, :], pC[0:16, :], EPS_AT, None, A.add, inc=False)
        v_rcs[tt] = v_op("reciprocal", rcpr[0:16, :], rcpr[0:16, :])
        if tt >= 1:
            vc.wait(tns, tn_qkv[tt - 1])
        v_rcc[tt] = v_op("tensor_copy", rcb[0:16, 0:512], rcpr[0:16, :])

    def emit_qkv(tt):
        qb_ = qbuf(tt)
        tn.wait(vcs, v_rcc[tt])
        for c in range(8):
            prb = pA if c % 2 == 0 else pB
            prev = v_qkv.get((tt, c - 2)) if c >= 2 else (
                v_qkv.get((tt - 1, 6 + c)) if tt >= 1 else None)
            pq = pK1 if c % 2 == 0 else pK2
            if prev:
                tn.wait(vcs, prev)
            mm(prb, selb[0:16, c * 128:(c + 1) * 128], rcb[0:16, 0:512], True, True)
            mm(pq[0:64, :], kvf[0:64, c * 64:(c + 1) * 64], qb_[0:64, c * 512:(c + 1) * 512],
               True, True, skip=True)
            mm(pq[64:128, :], kvf[64:128, c * 64:(c + 1) * 64],
               qb_[64:128, c * 512:(c + 1) * 512],
               True, True, inc=True, tp=(64, 64), skip=True)
            tn_q = tns.n
            vc.wait(tns, tn_q)
            rbs = bfr[:, (c % 2) * 1024 + 512:(c % 2) * 1024 + 1024][:, 0:512]
            v_op("tensor_copy", rbs, prb, inc=False)
            v_qkv[(tt, c)] = v_op("tensor_tensor",
                                  a1[:, c * T + tt * 512:c * T + (tt + 1) * 512], pq, rbs,
                                  A.mult)
        tn_qkv[tt] = tns.n

    tn.wait(scs, s_kpe[NT * 8 - 1])
    tn.wait(gpe, m_setup)
    gp.wait(tns, kv_all)
    for tt in range(NT):
        emit_qp(tt, mid=(lambda t_=tt - 1: emit_norm(t_)) if tt >= 1 else None)
        for i_ in range(tt * 8 // NT, (tt + 1) * 8 // NT):
            gp.dma(gpC, wvo[:, i_ * 1024:(i_ + 1) * 1024], dwo[i_ * 128:(i_ + 1) * 128, :])
        if tt >= 1:
            emit_qkv(tt - 1)
    emit_norm(NT - 1)
    emit_qkv(NT - 1)
    gp.wait(gpC)
    gpe.n += 1
    gp.do(lambda E: E.nop().then_inc(gpe.h, 1))
    m_wo = gpe.n
    v_qkv_all = vcs.n
    for _ in gen_a2:
        pass
    for _ in wq_state['gr1']:
        pass
    wq_state['gr2'] = round_w2_gen()
    s_wq_done = scs.n

    phase('B5')
    gp.wait(tns, kv_all)
    gp.wait(scs, s_wq_done)
    gp.wait(gpH0)
    gp.wait(gpH1)
    gp.wait(gpQ0)
    gp.wait(gpQ1)
    gp.wait(gpC)
    tn.wait(gpe, m_wo)
    tn.wait(vcs, v_qkv_all)
    v_x1 = {}
    s_xsq = {}
    sy_x1 = {}
    for t in range(TC):
        xr = fbuf[:, (t % 2) * 1024:(t % 2) * 1024 + 1024]
        gx = gpA if t % 2 == 0 else gpB
        if t >= 2:
            gp.wait(vcs, v_x1[t - 2])
        g_xr = gp.dma(gx, xr, dx[t * 128:(t + 1) * 128, :])
        pz0 = pC if t % 2 == 0 else pE
        pz1 = pD if t % 2 == 0 else pN
        if t >= 2:
            tn.wait(vcs, v_x1[t - 2])
        for dc in range(8):
            stat = a1[:, dc * T + t * 128:dc * T + (t + 1) * 128]
            mm(pz0, stat, wvo[:, dc * 1024:dc * 1024 + 512], dc == 0, dc == 7)
            mm(pz1, stat, wvo[:, dc * 1024 + 512:(dc + 1) * 1024], dc == 0, dc == 7,
               inc=(dc == 7))
        vc.wait(tns)
        vc.wait(gx, g_xr)
        if t == 0:
            vc.wait(gpH0)
            vc.wait(gpH1)
        if t >= 1:
            vc.wait(scs, s_xsq[t - 1])
        v_op("scalar_tensor_tensor", zb[:, 0:512], pz0, 1.0, bo_b[:, 0:512], A.mult, A.add,
             inc=False)
        v_op("scalar_tensor_tensor", zb[:, 512:1024], pz1, 1.0, bo_b[:, 512:1024],
             A.mult, A.add)
        x1b = ob[:, (t % 2) * 1024:(t % 2) * 1024 + 1024]
        sl = syA if t % 2 == 0 else syB
        if t >= 2:
            vc.wait(sl, sy_x1[t - 2])
        v_x1[t] = v_op("scalar_tensor_tensor", x1b, zb[:, :], 1.0, xr, A.mult, A.add,
                       accum_out=x_s[:, t:t + 1])
        v_op("reduce_max", x_mx[:, t:t + 1], x1b, AX, inc=False)
        v_op("tensor_reduce", x_mn[:, t:t + 1], x1b, AX, A.min)
        sc.wait(vcs, v_x1[t])
        s_xsq[t] = s_op("activation", zb[:, :], x1b, AF.Square, accum_out=x_q[:, t:t + 1])
        sy.wait(vcs, v_x1[t])
        sy_x1[t] = sy.dma(sl, s_x1[t * 128:(t + 1) * 128, :], x1b)
        wq_step_b5()
    s_x1sq = scs.n
    tn_b5 = tns.n
    m_x1st = sy_mark([syA, syB])

    vc.wait(scs, s_x1sq)
    v_op("tensor_scalar", x_m[:, :], x_s[:, :], 1.0 / 1024, None, A.mult, inc=False)
    v_op("tensor_tensor", x_v[:, :], x_m[:, :], x_m[:, :], A.mult, inc=False)
    v_op("scalar_tensor_tensor", x_v[:, :], x_q[:, :], 1.0 / 1024, x_v[:, :], A.mult,
         A.subtract)
    sc.wait(vcs)
    s_sd2 = s_op("activation", x_v[:, :], x_v[:, :], AF.Sqrt, bias=cst[:, 0:1])
    vc.wait(scs, s_sd2)
    v_op("reciprocal", x_r[:, :], x_v[:, :], inc=False)
    v_op("scalar_tensor_tensor", x_nm[:, :], x_m[:, :], -1.0, x_r[:, :], A.mult, A.mult,
         inc=False)
    v_op("tensor_tensor", st_t1[:, 0:TC], x_mx[:, :], x_m[:, :], A.subtract, inc=False)
    v_op("tensor_tensor", st_t2[:, 0:TC], x_m[:, :], x_mn[:, :], A.subtract, inc=False)
    v_op("tensor_tensor", st_t1[:, 0:TC], st_t1[:, 0:TC], st_t2[:, 0:TC], A.max, inc=False)
    v_op("tensor_tensor", st_t1[:, 0:TC], st_t1[:, 0:TC], x_r[:, :], A.mult, inc=False)
    v_lmr = v_op("reduce_max", st_t2[:, 0:1], st_t1[:, 0:TC], AX)
    sy.wait(vcs, v_lmr)
    sy.dma(syC, g128[:, :], st_t2[:, 0:1])
    m_g1 = sy_mark([syC])
    vc.wait(sye, m_g1)
    v_lmax = v_op("reduce_max", ccs[0:1, 0:1], g128[:, :], AX)

    phase('B6')
    v_z1 = v_op("memset", ccs[0:1, 1:8], 0.0)
    gp.wait(vcs, max(v_z1, v_lmax))
    gp.wait(gpC)
    gp.dma(gpC, cc1i.ap(), ccs[0:1, :])
    gp.wait(gpC)
    ccsm.n += 1
    gp.do(lambda E: E.collective_compute(
        "AllReduce", A.max, replica_groups=[list(range(ncores))],
        ins=[cc1i.ap().opt()], outs=[cc1o.ap().opt()]).then_inc(ccsm.h, 1))
    gp.wait(ccsm)
    gp.dma(gpC, ccs[0:1, 0:1], cc1o.ap()[0:1, 0:1])
    m_m1 = gp_mark()
    vc.wait(gpe, m_m1)
    v_op("reciprocal", sc1[0:1, 8:9], ccs[0:1, 0:1], inc=False)
    v_op("tensor_scalar", sc1[0:1, 8:9], sc1[0:1, 8:9], QP, None, A.mult, inc=False)
    v_op("tensor_scalar", sc1[0:1, 9:10], ccs[0:1, 0:1], 1.0 / QP, None, A.mult, inc=False)
    v_op("tensor_tensor", sc1[0:1, 10:11], sc1[0:1, 6:7], sc1[0:1, 9:10], A.mult)
    sy.wait(vcs)
    sy.dma(syC, s_fl[1:2, :], sc1[0:1, 8:16])
    m_fl1 = sy_mark([syC])
    gp.wait(sye, m_fl1)
    gp.dma(gpC, ppc[:, 0:1], s_fl[1:2, 0:1].partition_broadcast(128))
    gp.dma(gpC, ppc[:, 1:2], s_fl[1:2, 2:3].partition_broadcast(128))
    m_c1 = gp_mark()
    vc.wait(gpe, m_c1)
    v_op("tensor_scalar", rs2[:, :], x_r[:, :], ppc[:, 0:1], None, A.mult, inc=False)
    v_op("tensor_scalar", boff[:, :], x_nm[:, :], ppc[:, 0:1], None, A.mult, inc=False)
    vc.wait(sye, m_bias)
    v_b1 = v_op("tensor_scalar", c1b1[:, :], b1c[:, :], ppc[:, 1:2], None, A.mult)

    for _ in wq_state['gr2']:
        pass

    phase('B7')
    s_x = {}
    v_x = {}
    sy_xs = {}
    m_xw = {}
    gws = {}
    tn_w1 = {}

    def load_wslot(hc_):
        wslot_ = wvo[:, (hc_ % 2) * 1024:(hc_ % 2) * 1024 + 1024]
        ss_ = syS0 if hc_ % 2 == 0 else syS1
        if hc_ >= 2:
            sy.wait(tns, tn_w1[hc_ - 2])
        for dc_ in range(8):
            gws[hc_] = sy.dma_tr(ss_, wslot_[:, dc_ * 128:(dc_ + 1) * 128],
                                 s_w1q[hc_ * 128:(hc_ + 1) * 128,
                                       dc_ * 128:(dc_ + 1) * 128])

    sy.wait(tns, tn_b5)
    sy.wait(syW0)
    sy.wait(syW1)
    load_wslot(0)
    if HC > 1:
        load_wslot(1)
    gp.wait(sye, m_x1st)
    gp.wait(scs)
    gp.wait(vcs)
    for t in range(TC):
        xb = fbuf[:, (t % 2) * 1024:(t % 2) * 1024 + 1024]
        gx = gpA if t % 2 == 0 else gpB
        if t >= 2:
            gp.wait(scs, s_x[t - 2])
        g = gp.dma(gx, xb, s_x1[t * 128:(t + 1) * 128, :])
        sc.wait(gx, g)
        sc.wait(vcs, v_b1)
        tb = fbuf[:, 2048:3072] if t % 2 == 0 else fbuf[:, 3072:4096]
        if t >= 2:
            sc.wait(vcs, v_x[t - 2])
        s_x[t] = s_op("activation", tb, xb, AF.Identity, bias=boff[:, t:t + 1],
                      scale=rs2[:, t:t + 1])
        vc.wait(scs, s_x[t])
        qb = bfr[:, (t % 2) * 1024:(t % 2) * 1024 + 1024]
        if t == 0:
            vc.wait(syK0)
            vc.wait(syK1)
        sx = scX0 if t % 2 == 0 else scX1
        if t >= 2:
            vc.wait(sx, sy_xs[t - 2])
        v_x[t] = v_op("tensor_scalar", qb, tb, MAGIC, -MAGIC, A.add, A.add)
        sc.wait(vcs, v_x[t])
        sy_xs[t] = sc.dma(sx, s_xq[t * 128:(t + 1) * 128, :], qb)
        if t % 4 == 3:
            w = t // 4
            sy.wait(scX0)
            sy.wait(scX1)
            if w == 0:
                sy.wait(tns, tn_b5)
            for dc in range(8):
                sy.dma_tr(syC, a1[:, dc * T + w * 512:dc * T + (w + 1) * 512],
                          s_xq[w * 512:(w + 1) * 512, dc * 128:(dc + 1) * 128])
            m_xw[w] = sy_mark([syC])

    w2t = []
    for hc in range(HC):
        if hc < 8:
            w2t.append(wqr[:, hc * 1024:(hc + 1) * 1024])
        elif hc < 16:
            w2t.append(wkr[:, (hc - 8) * 1024:(hc - 8 + 1) * 1024])
        elif hc < 24:
            w2t.append(ring[:, (hc - 16) * 1024:(hc - 16 + 1) * 1024])
        elif hc < 28:
            w2t.append(qpr[:, (hc - 24) * 1024:(hc - 24 + 1) * 1024])
        else:
            w2t.append(a1[:, (hc - 28) * 1024:(hc - 28 + 1) * 1024])

    s_gel = {}
    gp_hst = {}
    v_hm = {}
    n_w2t_done = 0
    for hc in range(HC):
        wslot = wvo[:, (hc % 2) * 1024:(hc % 2) * 1024 + 1024]
        ss = syS0 if hc % 2 == 0 else syS1
        if n_w2t_done < min(28, HC):
            sy.dma_tr(syC, w2t[n_w2t_done], s_w2q[:, n_w2t_done * 128:(n_w2t_done + 1) * 128])
            n_w2t_done += 1
        tn.wait(ss, gws[hc])
        for tt in range(NT):
            idx = hc * NT + tt
            p_ = pA if idx % 2 == 0 else pB
            if hc == 0:
                tn.wait(sye, m_xw[tt])
            if idx >= 2:
                tn.wait(scs, s_gel[idx - 2])
            for dc in range(8):
                mm(p_, wslot[:, dc * 128:(dc + 1) * 128],
                   a1[:, dc * T + tt * 512:dc * T + (tt + 1) * 512],
                   dc == 0, dc == 7, inc=(dc == 7))
            sc.wait(tns)
            hb = fbuf[:, (idx % 2) * 512:(idx % 2) * 512 + 512]
            gh = gpH0 if idx % 2 == 0 else gpH1
            if idx >= 2:
                sc.wait(gh, gp_hst[idx - 2])
                sc.wait(vcs, v_hm[idx - 2])
            s_gel[idx] = s_op("activation", hb, p_, AF.Gelu, bias=c1b1[:, hc:hc + 1],
                              scale=ppc[:, 1:2])
            vc.wait(scs, s_gel[idx])
            v_hm[idx] = v_op("reduce_max", hmaxs[:, idx:idx + 1], hb, AX,
                             apply_absolute_value=True)
            gp.wait(scs, s_gel[idx])
            gp_hst[idx] = gp.dma(gh, s_h[hc * 128:(hc + 1) * 128, tt * 512:(tt + 1) * 512], hb)
        tn_w1[hc] = tns.n
        if hc + 2 < HC:
            load_wslot(hc + 2)
    while n_w2t_done < min(28, HC):
        sy.dma_tr(syC, w2t[n_w2t_done], s_w2q[:, n_w2t_done * 128:(n_w2t_done + 1) * 128])
        n_w2t_done += 1
    m_w2ta = sy_mark([syC, syS0, syS1])

    phase('B8')
    v_op("reduce_max", st_t2[:, 0:1], hmaxs[:, 0:HC * NT], AX)
    sy.wait(vcs)
    sy.dma(syC, g128[:, :], st_t2[:, 0:1])
    m_g4 = sy_mark([syC])
    vc.wait(sye, m_g4)
    v_lm2 = v_op("reduce_max", ccs[0:1, 0:1], g128[:, :], AX)
    v_z2 = v_op("memset", ccs[0:1, 1:8], 0.0)
    gp.wait(vcs, max(v_lm2, v_z2))
    gp.wait(gpH0)
    gp.wait(gpH1)
    gp.wait(scs)
    gp.wait(vcs)
    hld = {}
    hld[0] = gp.dma(gpA, fbuf[:, 0:1024],
                    bass.AP(s_hh, 0, [[T, 128], [128 * T, 8], [1, 128]]))
    _q1, _t1 = (1 % (HID // 1024), 1 // (HID // 1024)) if HID >= 1024 else (0, 1)
    hld[1] = gp.dma(gpB, fbuf[:, 1024:2048],
                    bass.AP(s_hh, (_q1 * 8) * 128 * T + _t1 * 128,
                            [[T, 128], [128 * T, 8], [1, 128]]))
    gp.wait(gpC)
    gp.dma(gpC, cc2i.ap(), ccs[0:1, :])
    gp.wait(gpC)
    ccsm.n += 1
    gp.do(lambda E: E.collective_compute(
        "AllReduce", A.max, replica_groups=[list(range(ncores))],
        ins=[cc2i.ap().opt()], outs=[cc2o.ap().opt()]).then_inc(ccsm.h, 1))
    gp.wait(ccsm)
    gp.dma(gpC, ccs[0:1, 0:1], cc2o.ap()[0:1, 0:1])
    m_m2 = gp_mark()
    vc.wait(gpe, m_m2)
    v_op("reciprocal", sc1[0:1, 8:9], ccs[0:1, 0:1], inc=False)
    v_op("tensor_scalar", sc1[0:1, 8:9], sc1[0:1, 8:9], QP, None, A.mult, inc=False)
    v_op("tensor_scalar", sc1[0:1, 9:10], ccs[0:1, 0:1], 1.0 / QP, None, A.mult, inc=False)
    v_op("tensor_tensor", sc1[0:1, 10:11], sc1[0:1, 7:8], sc1[0:1, 9:10], A.mult)
    sy.wait(vcs)
    sy.dma(syC, s_fl[2:3, :], sc1[0:1, 8:16])
    m_fl2 = sy_mark([syC])
    gp.wait(sye, m_fl2)
    gp.dma(gpC, ppc[:, 2:3], s_fl[2:3, 0:1].partition_broadcast(128))
    gp.dma(gpC, ppc[:, 3:4], s_fl[2:3, 2:3].partition_broadcast(128))
    m_c2 = gp_mark()
    vc.wait(gpe, m_c2)
    v_b2 = v_op("tensor_scalar", bo_b[:, :], b2_b[:, :], ppc[:, 3:4], None, A.mult)
    sy.wait(tns)
    for hc in range(min(28, HC), HC):
        sy.dma_tr(syC, w2t[hc], s_w2q[:, hc * 128:(hc + 1) * 128])
    m_w2t = sy_mark([syC])

    phase('B9')
    tn.wait(sye, m_w2t)
    tn.wait(sye, m_w2ta)
    tn.wait(vcs, v_b2)
    NQ = HID // 1024
    s_hq = {}
    v_hq = {}
    tn_hqu = {}
    v_out = {}
    sy_o = {}
    g_x1r = {}
    gp.wait(scs)
    gp.wait(vcs)
    gp.wait(gpH0)
    gp.wait(gpH1)
    gp.wait(gpQ0)
    gp.wait(gpQ1)
    for tt in range(TC):
        po0 = pC if tt % 2 == 0 else pE
        po1 = pD if tt % 2 == 0 else pN
        outb = ob[:, (tt % 2) * 1024:(tt % 2) * 1024 + 1024]
        gxr = gpR if tt % 2 == 0 else gpP
        sl = syA if tt % 2 == 0 else syB
        if tt >= 2:
            gp.wait(vcs, v_out[tt - 2])
            gp.wait(sl, sy_o[tt - 2])
        g_x1r[tt] = gp.dma(gxr, outb, s_x1[tt * 128:(tt + 1) * 128, :])
        for q in range(NQ):
            qidx = tt * NQ + q
            hb = fbuf[:, (qidx % 2) * 1024:(qidx % 2) * 1024 + 1024]
            gx = gpA if qidx % 2 == 0 else gpB
            if qidx < 2:
                g = hld[qidx]
            else:
                gp.wait(scs, s_hq[qidx - 2])
                g = gp.dma(gx, hb,
                           bass.AP(s_hh, (q * 8) * 128 * T + tt * 128,
                                   [[T, 128], [128 * T, 8], [1, 128]]))
            sc.wait(gx, g)
            sc.wait(gpe, m_c2)
            tb = fbuf[:, 2048:3072] if qidx % 2 == 0 else rcpb[:, 0:1024]
            if qidx >= 2:
                sc.wait(vcs, v_hq[qidx - 2])
            s_hq[qidx] = s_op("activation", tb, hb, AF.Identity, bias=cst[:, 1:2],
                              scale=ppc[:, 2:3])
            vc.wait(scs, s_hq[qidx])
            qb = bfr[:, (qidx % 2) * 1024:(qidx % 2) * 1024 + 1024]
            if qidx == 0:
                vc.wait(scX0)
                vc.wait(scX1)
            if qidx >= 2:
                vc.wait(tns, tn_hqu[qidx - 2])
            v_hq[qidx] = v_op("tensor_scalar", qb, tb, -MAGIC, None, A.add)
            if q == 0 and tt >= 2:
                tn.wait(vcs, v_out[tt - 2])
            tn.wait(vcs, v_hq[qidx])
            for hcq in range(8):
                hc = q * 8 + hcq
                first = (q == 0 and hcq == 0)
                last = (q == NQ - 1 and hcq == 7)
                mm(po0, qb[:, hcq * 128:(hcq + 1) * 128], w2t[hc][:, 0:512], first, last)
                mm(po1, qb[:, hcq * 128:(hcq + 1) * 128], w2t[hc][:, 512:1024], first, last,
                   inc=(hcq == 7))
            tn_hqu[qidx] = tns.n
        vc.wait(tns)
        vc.wait(gxr, g_x1r[tt])
        v_op("scalar_tensor_tensor", zb[:, 0:512], po0, ppc[:, 3:4], bo_b[:, 0:512],
             A.mult, A.add, inc=False)
        v_op("scalar_tensor_tensor", zb[:, 512:1024], po1, ppc[:, 3:4], bo_b[:, 512:1024],
             A.mult, A.add)
        v_out[tt] = v_op("tensor_tensor", outb, zb[:, :], outb, A.add)
        sy.wait(vcs, v_out[tt])
        sy_o[tt] = sy.dma(sl, dout[tt * 128:(tt + 1) * 128, :], outb)
    sy.wait(syA)
    sy.wait(syB)

    phase('END')
    if cut is not None:
        lens = phase_marks[cut]
        for e, L in zip((gp, sy, tn, sc, vc), lens):
            del e.ops[L:]
    with contextlib.ExitStack() as st_:
        for s, nm in ALL_SEMS:
            s.h = st_.enter_context(nc.semaphore(nm))
        with nc.Block() as block:
            @block.gpsimd
            def _(E):
                gp.run(E)

            @block.sync
            def _(E):
                sy.run(E)

            @block.tensor
            def _(E):
                tn.run(E)

            @block.scalar
            def _(E):
                sc.run(E)

            @block.vector
            def _(E):
                vc.run(E)
    return nc


def kernel(L0_attn_x, L0_attn_Wq, L0_attn_bq, L0_attn_Wk, L0_attn_bk, L0_attn_Wv, L0_attn_bv,
           L0_attn_Wo, L0_attn_bo, L0_attn_rw, L0_ffn_w1, L0_ffn_b1, L0_ffn_w2, L0_ffn_b2):
    n = 8
    x = np.ascontiguousarray(np.asarray(L0_attn_x, dtype=np.float32))
    B, S, D = x.shape
    nc = build(T=S, HID=int(L0_ffn_w1.shape[0]), ncores=n)
    com = {
        "wq": np.ascontiguousarray(L0_attn_Wq, dtype=np.float32),
        "wk": np.ascontiguousarray(L0_attn_Wk, dtype=np.float32),
        "wv": np.ascontiguousarray(L0_attn_Wv, dtype=np.float32),
        "wo": np.ascontiguousarray(L0_attn_Wo, dtype=np.float32),
        "bq": np.ascontiguousarray(L0_attn_bq, dtype=np.float32).reshape(1, -1),
        "bk": np.ascontiguousarray(L0_attn_bk, dtype=np.float32).reshape(1, -1),
        "bv": np.ascontiguousarray(L0_attn_bv, dtype=np.float32).reshape(1, -1),
        "bo": np.ascontiguousarray(L0_attn_bo, dtype=np.float32).reshape(1, -1),
        "rw": np.ascontiguousarray(L0_attn_rw, dtype=np.float32),
        "w1": np.ascontiguousarray(L0_ffn_w1, dtype=np.float32),
        "b1": np.ascontiguousarray(L0_ffn_b1, dtype=np.float32).reshape(1, -1),
        "w2": np.ascontiguousarray(L0_ffn_w2, dtype=np.float32),
        "b2": np.ascontiguousarray(L0_ffn_b2, dtype=np.float32).reshape(1, -1),
    }
    in_maps = [dict(com, x=np.ascontiguousarray(x[b])) for b in range(n)]
    res = run_bass_kernel_spmd(nc, in_maps, core_ids=list(range(n)))
    return np.stack([np.asarray(r["out"], dtype=np.float32) for r in res.results], axis=0)
